# version 04
# baseline (speedup 1.0000x reference)
import math
import numpy as np

import concourse.bass as bass
import concourse.mybir as mybir
from concourse.bass_utils import run_bass_kernel_spmd
from concourse.bass_isa import ReduceOp
from concourse import library_config as libcfg

F32 = mybir.dt.float32
I16 = mybir.dt.int16
U32 = mybir.dt.uint32
Alu = mybir.AluOpType
AX = mybir.AxisListType
Act = mybir.ActivationFunctionType

ENGINES = ("sync", "vector", "scalar", "tensor", "gpsimd")


class Cfg:
    def __init__(self, D=1024, S=8192, E=8, I=4096, CAP=1152, TCH=384,
                 KB=512, NITER=56, SHIFT=12.0, EPS=1e-5):
        self.D, self.S, self.E, self.I = D, S, E, I
        self.SO = S // 2
        self.CAP, self.TCH = CAP, TCH
        self.KB = KB
        self.NITER = NITER
        self.SHIFT = SHIFT
        self.EPS = EPS
        self.DC = D // 128
        self.IC = I // 128
        self.NT = S // 128
        self.NCH = S // 512
        self.NKB = S // KB
        self.NTCH = CAP // TCH
        self.KSEL = 2 * S // E
        nqt = self.SO // 128
        self.QCS = []
        while nqt > 0:
            t = min(3, nqt)
            self.QCS.append(t)
            nqt -= t


class Prog:

    def __init__(self):
        self.items = {e: [] for e in ENGINES}
        self.counts = {}
        self.last_wait = {e: {} for e in ENGINES}

    def val(self, sem):
        return self.counts.get(sem, 0)

    def do(self, eng, fn, wait=(), inc=None, incn=1, ss=False):
        w = []
        for s, v in wait:
            if v <= 0:
                continue
            if self.last_wait[eng].get(s, -1) >= v:
                continue
            self.last_wait[eng][s] = v
            w.append((s, v))
        post = []
        if inc is not None:
            self.counts[inc] = self.counts.get(inc, 0) + incn
            post.append((inc, incn))
        if ss and inc is None:
            sn = "ss_" + eng
            self.counts[sn] = self.counts.get(sn, 0) + 1
            post.append((sn, 1))
        self.items[eng].append((tuple(w), fn, tuple(post)))
        if ss:
            sw, svv = post[-1][0], self.counts[post[-1][0]]
            self.items[eng].append((((sw, svv),), None, ()))
            self.last_wait[eng][sw] = svv
        return self.counts.get(inc, 0) if inc else None

    def wait(self, eng, sem, v):
        self.do(eng, None, wait=((sem, v),))


def build_program(cfg, debug=False, pend=4):
    nc = bass.Bass("TRN2", target_bir_lowering=False, detect_race_conditions=False)
    D, S, E, I = cfg.D, cfg.S, cfg.E, cfg.I
    SO, CAP, TCH, KB = cfg.SO, cfg.CAP, cfg.TCH, cfg.KB
    DC, IC, NT, NCH, NKB, NTCH = cfg.DC, cfg.IC, cfg.NT, cfg.NCH, cfg.NKB, cfg.NTCH
    F = 3 * D
    JH = SO // 128
    JF = NT
    ICOLS = CAP // 16
    ICC = TCH // 16
    TPC = TCH // 128
    NKC = KB // 128
    NHV = max(1, D // 512)
    NW = min(512, D)

    xf = nc.dram_tensor("xf", [S, D], F32, kind="ExternalInput").ap()
    wqkv = nc.dram_tensor("wqkv", [D, F], F32, kind="ExternalInput").ap()
    bqkv = nc.dram_tensor("bqkv", [F], F32, kind="ExternalInput").ap()
    wout = nc.dram_tensor("wout", [D, D], F32, kind="ExternalInput").ap()
    bout = nc.dram_tensor("bout", [D], F32, kind="ExternalInput").ap()
    gwT = nc.dram_tensor("gwT", [D, E], F32, kind="ExternalInput").ap()
    wgT = nc.dram_tensor("wgT", [E, IC, 128, DC, 128], F32, kind="ExternalInput").ap()
    wuT = nc.dram_tensor("wuT", [E, IC, 128, DC, 128], F32, kind="ExternalInput").ap()
    wdT = nc.dram_tensor("wdT", [E, DC, 128, IC, 128], F32, kind="ExternalInput").ap()
    out = nc.dram_tensor("out", [SO + 128, D], F32, kind="ExternalOutput").ap()

    hbm_kind = "ExternalOutput" if debug else "Internal"
    xn_h = nc.dram_tensor("xn_h", [S, D], F32, kind=hbm_kind).ap()
    af_h = nc.dram_tensor("af_h", [S, 64], F32, kind=hbm_kind).ap()
    qT_h = nc.dram_tensor("qT_h", [DC, 128, SO], F32, kind=hbm_kind).ap()
    kT_h = nc.dram_tensor("kT_h", [DC, 128, S], F32, kind=hbm_kind).ap()
    v_h = nc.dram_tensor("v_h", [S, D], F32, kind=hbm_kind).ap()
    if debug:
        dbg_aff = nc.dram_tensor("dbg_aff", [128, E * JF], F32, kind="ExternalOutput").ap()
        dbg_lo = nc.dram_tensor("dbg_lo", [128, E], F32, kind="ExternalOutput").ap()
        dbg_nf = nc.dram_tensor("dbg_nf", [1, E], U32, kind="ExternalOutput").ap()
        dbg_idx = nc.dram_tensor("dbg_idx", [128, E * ICOLS], I16, kind="ExternalOutput").ap()
        dbg_st1 = nc.dram_tensor("dbg_st1", [128, 5 * NT], F32, kind="ExternalOutput").ap()
        dbg_bi = nc.dram_tensor("dbg_bi", [128, 3 * E], F32, kind="ExternalOutput").ap()
        dbg_cx = nc.dram_tensor("dbg_cx", [2, 128, 3 * D], F32, kind="ExternalOutput").ap()
        dbg_SE = nc.dram_tensor("dbg_SE", [2, 128, KB], F32, kind="ExternalOutput").ap()
        dbg_el = nc.dram_tensor("dbg_el", [2, 128, 3 * 17], F32, kind="ExternalOutput").ap()

    A = nc.alloc_sbuf_tensor
    ident = A("ident", [128, 128], F32).ap()
    onesc = A("onesc", [1, 128], F32).ap()
    scols = A("scols", [128, NT], F32).ap()
    qcols = A("qcols", [128, NT], F32).ap()
    mcols = A("mcols", [128, NT], F32).ap()
    icols = A("icols", [128, NT], F32).ap()
    tmp1 = A("tmp1", [128, max(NT, 8)], F32).ap()
    tmp2 = A("tmp2", [128, max(NT, 8)], F32).ap()
    aff_tm = A("aff_tm", [128, E * JF], F32).ap()
    bqk_sb = A("bqk_sb", [128, 2 * DC], F32).ap()
    bv_sb = A("bv_sb", [1, D], F32).ap()
    bo_sb = A("bo_sb", [1, D], F32).ap()
    gw_sb = A("gw_sb", [128, DC * E], F32).ap()
    lo = A("lo", [128, E], F32).ap()
    hi = A("hi", [128, E], F32).ap()
    mid = A("mid", [128, E], F32).ap()
    cnt = A("cnt", [128, E], F32).ap()
    pred = A("pred", [128, E], F32).ap()
    dd1 = A("dd1", [128, E], F32).ap()
    msk = A("msk", [128, E * JF], F32).ap()
    iota1 = A("iota1", [128, E * JH], F32).ap()
    mif = A("mif", [128, E * JH], F32).ap()
    mi16 = A("mi16", [128, E * JH], I16).ap()
    wrap = A("wrap", [16, E * 8 * JH], I16).ap()
    idxs = A("idxs", [128, E * ICOLS], I16).ap()
    gidx = A("gidx", [128, E * ICOLS], I16).ap()
    sidx = A("sidx", [128, E * ICOLS], I16).ap()
    smsk = A("smsk", [128, E * ICOLS], I16).ap()
    nfound = A("nfound", [16, E], U32).ap()
    sio = A("sio", [16, ICOLS], F32).ap()
    Lcol = A("Lcol", [128, 3 * 16], F32).ap()
    ell = A("ell", [128, 3], F32).ap()
    sq_jnk = A("sq_jnk", [128, 1024], F32).ap()
    shf_sb = A("shf_sb", [128, 1], F32).ap()

    AR = 176 * 1024
    arena = A("arena", [128, AR], mybir.dt.uint8)
    abase = nc.lookup_mloc(arena).addr

    def at(name, shape, dtype, off):
        assert off + int(np.prod(shape[1:])) * mybir.dt.size(dtype) <= AR, name
        return nc.alloc_sbuf_tensor_at(name, shape, dtype, offset=abase + off).ap()

    fb = 4
    o = 0
    wq_sb = at("wq_sb", [128, DC * F], F32, o); o += DC * F * fb
    xc_sb = []
    for b in range(2):
        xc_sb.append(at(f"xc{b}", [128, 4 * D], F32, o)); o += 4 * D * fb
    xn_sb = at("xn_sb", [128, 4 * D], F32, o); o += 4 * D * fb
    xnT_sb = at("xnT_sb", [128, DC * 512], F32, o); o += DC * 512 * fb
    qke_sb = []
    for b in range(2):
        qke_sb.append(at(f"qke{b}", [128, 512], F32, o)); o += 512 * fb
    ve_sb = []
    for b in range(2):
        ve_sb.append(at(f"ve{b}", [128, D], F32, o)); o += D * fb
    afp_sb = at("afp_sb", [128, 4 * 64], F32, o); o += 4 * 64 * fb
    afc_sb = at("afc_sb", [128, 4 * E], F32, o); o += 4 * E * fb

    o = 0
    wo_sb = at("wo_sb", [128, DC * D], F32, o); o += DC * D * fb
    qt_sb = at("qt_sb", [128, DC * 384], F32, o); o += DC * 384 * fb
    kt_sb = []
    for b in range(2):
        kt_sb.append(at(f"kt{b}", [128, DC * KB], F32, o)); o += DC * KB * fb
    vb_sb = []
    for b in range(2):
        vb_sb.append(at(f"vb{b}", [128, NKC * D], F32, o)); o += NKC * D * fb
    E_sb = at("E_sb", [128, 3 * KB], F32, o); o += 3 * KB * fb
    et_sb = at("et_sb", [128, NKC * 384], F32, o); o += NKC * 384 * fb
    cx_sb = at("cx_sb", [128, 3 * D], F32, o); o += 3 * D * fb
    cxT_sb = at("cxT_sb", [128, DC * 384], F32, o); o += DC * 384 * fb
    xr_sb = at("xr_sb", [128, 3 * D], F32, o); o += 3 * D * fb
    ou_sb = at("ou_sb", [128, 3 * D], F32, o); o += 3 * D * fb

    o = 0
    gb_sb = []
    for b in range(2):
        gb_sb.append(at(f"gb{b}", [128, TPC * D], F32, o)); o += TPC * D * fb
    ab_sb = []
    for b in range(2):
        ab_sb.append(at(f"ab{b}", [128, TPC * 64], F32, o)); o += TPC * 64 * fb
    BF16 = mybir.dt.bfloat16
    o = (o + 31) // 32 * 32
    wstrip = []
    for b in range(2):
        wstrip.append(at(f"wstrip{b}", [128, 512], F32, o)); o += 512 * fb
    wgh_sb = []; wgl_sb = []; wuh_sb = []; wul_sb = []; wdh_sb = []; wdl_sb = []
    for b2 in range(2):
        wgh_sb.append(at(f"wgh{b2}", [128, DC * 128], BF16, o)); o += DC * 128 * 2
        wgl_sb.append(at(f"wgl{b2}", [128, DC * 128], BF16, o)); o += DC * 128 * 2
        wuh_sb.append(at(f"wuh{b2}", [128, DC * 128], BF16, o)); o += DC * 128 * 2
        wul_sb.append(at(f"wul{b2}", [128, DC * 128], BF16, o)); o += DC * 128 * 2
        wdh_sb.append(at(f"wdh{b2}", [128, IC * 128], BF16, o)); o += IC * 128 * 2
        wdl_sb.append(at(f"wdl{b2}", [128, IC * 128], BF16, o)); o += IC * 128 * 2
    xgh_sb = at("xgh_sb", [128, DC * TCH], BF16, o); o += DC * TCH * 2
    xgl_sb = at("xgl_sb", [128, DC * TCH], BF16, o); o += DC * TCH * 2
    hh_sb = at("hh_sb", [128, IC * TCH], BF16, o); o += IC * TCH * 2
    hl_sb = at("hl_sb", [128, IC * TCH], BF16, o); o += IC * TCH * 2
    o = (o + 31) // 32 * 32
    h1_sb = at("h1_sb", [128, TCH], F32, o); o += TCH * fb
    yt_sb = at("yt_sb", [128, DC * TCH], F32, o); o += DC * TCH * fb
    y_sb = at("y_sb", [128, TPC * D], F32, o); o += TPC * D * fb
    ga_sb = at("ga_sb", [128, TPC], F32, o); o += ((TPC * fb + 31) // 32) * 32
    WSTW = 512
    wscr = at("wscr", [128, WSTW], F32, o); o += WSTW * fb

    p01 = nc.alloc_psum_tensor("p01", [128, 1024], F32).ap()
    p23 = nc.alloc_psum_tensor("p23", [128, 1024], F32).ap()
    p45 = nc.alloc_psum_tensor("p45", [128, 1024], F32).ap()
    p67 = nc.alloc_psum_tensor("p67", [128, 1024], F32).ap()

    P = Prog()
    V = P.val

    def mk(meth, *args, **kw):
        def fn(e):
            try:
                return getattr(e, meth)(*args, **kw)
            except Exception as ex:
                raise RuntimeError(f"emit failed: {meth} args={[getattr(a,'shape',a) for a in args]}") from ex
        return fn

    P.do("gpsimd", mk("load_library", libcfg.standard))
    P.do("gpsimd", mk("memset", ident[:, :], 1.0))
    P.do("gpsimd", mk("affine_select", ident[:, :], ident[:, :], [[-1, 128]],
                      Alu.is_equal, 0.0, base=0, channel_multiplier=1), inc="id_g")
    P.do("gpsimd", mk("iota", iota1[:, :], [[0, E], [128, JH]], base=1,
                      channel_multiplier=1, allow_small_or_imprecise_dtypes=True),
         inc="id_g")
    P.do("gpsimd", mk("iota", sio[:, :], [[16, ICOLS]], base=0,
                      channel_multiplier=1, allow_small_or_imprecise_dtypes=True),
         inc="id_g")
    P.do("vector", mk("memset", onesc[:, :], 1.0), inc="c_v")
    P.do("vector", mk("memset", gidx[:, :], 0), inc="c_v")
    P.do("vector", mk("memset", sidx[:, :], 0), inc="c_v")
    P.do("vector", mk("memset", Lcol[:, :], 0.0), inc="c_v")
    P.do("vector", mk("memset", shf_sb[:, :], -cfg.SHIFT), inc="c_v")
    P.do("vector", mk("memset", afp_sb[:, :], 0.0), inc="c_v")
    P.do("vector", mk("memset", idxs[:, :], 0), inc="c_v")
    P.do("vector", mk("memset", smsk[:, :], 0), inc="c_v")
    P.do("sync", mk("dma_start", out=bqk_sb[:, :],
                    in_=bqkv[0:2 * D].rearrange("(ft p) -> p ft", p=128),
                    allow_slow_non_contiguous=True), inc="c_ld", incn=16)
    P.do("sync", mk("dma_start", out=bv_sb[:, :], in_=bqkv[2 * D:3 * D][None, :]), inc="c_ld", incn=16)
    P.do("sync", mk("dma_start", out=bo_sb[:, :], in_=bout[:][None, :]), inc="c_ld", incn=16)
    P.do("sync", mk("dma_start", out=gw_sb[:, :].rearrange("p (dc e) -> p dc e", e=E),
                    in_=gwT[:, :].rearrange("(dc p) e -> p dc e", p=128)), inc="c_ld", incn=16)
    P.do("sync", mk("dma_start", out=wq_sb[:, :].rearrange("p (dc f) -> p dc f", dc=DC),
                    in_=wqkv[:, :].rearrange("(dc p) f -> p dc f", p=128)), inc="w_ld", incn=16)

    for j in range(NT):
        b = j % 2
        xt = xc_sb[b][:, 0:D]
        xls = f"x0l{j % 2}"
        P.do("sync", mk("dma_start", out=xt, in_=xf[128 * j:128 * j + 128, :]),
             wait=(("st_v", max(0, j - 1)), ("st_s", max(0, j - 1))), inc=xls, incn=16)
        P.do("vector", mk("reduce_sum", scols[:, j:j + 1], xt, axis=AX.X),
             wait=((xls, V(xls)),), inc="st_v")
        P.do("scalar", mk("activation", sq_jnk[:, 0:D], xt, Act.Square,
                          bias=0.0, scale=1.0, accum_out=qcols[:, j:j + 1]),
             wait=((xls, V(xls)),), inc="st_s")
    st_all = V("st_v")
    sts_all = V("st_s")
    P.do("vector", mk("tensor_scalar", mcols[:, :], scols[:, :], 1.0 / D, None, Alu.mult),
         wait=(("st_v", st_all), ("st_s", sts_all)))
    P.do("vector", mk("tensor_scalar", tmp1[:, 0:NT], qcols[:, :], 1.0 / D, None, Alu.mult))
    P.do("vector", mk("tensor_tensor", tmp2[:, 0:NT], mcols[:, :], mcols[:, :], Alu.mult))
    P.do("vector", mk("tensor_tensor", tmp1[:, 0:NT], tmp1[:, 0:NT], tmp2[:, 0:NT], Alu.subtract))
    P.do("vector", mk("tensor_scalar", tmp1[:, 0:NT], tmp1[:, 0:NT], cfg.EPS, None, Alu.add),
         inc="var_v")
    P.do("scalar", mk("activation", tmp2[:, 0:NT], tmp1[:, 0:NT], Act.Sqrt, bias=0.0, scale=1.0),
         wait=(("var_v", 1),), inc="sqr_sc")
    P.do("vector", mk("reciprocal", icols[:, :], tmp2[:, 0:NT]), wait=(("sqr_sc", 1),),
         inc="vself")
    P.wait("vector", "vself", V("vself"))
    for _ in range(2):
        P.do("vector", mk("tensor_tensor", tmp2[:, 0:NT], icols[:, :], icols[:, :], Alu.mult))
        P.do("vector", mk("tensor_tensor", tmp2[:, 0:NT], tmp2[:, 0:NT], tmp1[:, 0:NT], Alu.mult))
        P.do("vector", mk("tensor_scalar", tmp2[:, 0:NT], tmp2[:, 0:NT], -0.5, 1.5, Alu.mult, Alu.add))
        P.do("vector", mk("tensor_tensor", icols[:, :], icols[:, :], tmp2[:, 0:NT], Alu.mult))
    P.do("vector", mk("engine_nop"), inc="is_v")
    if debug:
        for di, src_t in enumerate((scols, qcols, mcols, icols, tmp1)):
            P.do("scalar", mk("dma_start", out=dbg_st1[:, di * NT:(di + 1) * NT],
                              in_=src_t[:, 0:NT]),
                 wait=(("is_v", 1),), inc="dbg_st", incn=16)
        P.wait("scalar", "dbg_st", V("dbg_st"))

    if pend >= 1:
        xnst_hist = [0] * (NCH + 1)
        qst_hist = []
        vst_hist = []
        for ci in range(NCH):
            cb = ci % 2
            own = (512 * ci) < SO
            for jj in range(4):
                j = 4 * ci + jj
                P.do("sync", mk("dma_start", out=xc_sb[cb][:, jj * D:(jj + 1) * D],
                                in_=xf[128 * j:128 * j + 128, :]),
                     wait=(("st_v", st_all if ci <= 1 else 0),
                           ("st_s", sts_all if ci <= 1 else 0),
                           ("xn_v", max(0, 4 * (ci - 1))),), inc=f"xl{ci % 2}", incn=16)
            tp_before = V("tp_mm")
            for jj in range(4):
                j = 4 * ci + jj
                P.do("vector", mk("tensor_scalar", xn_sb[:, jj * D:(jj + 1) * D],
                                  xc_sb[cb][:, jj * D:(jj + 1) * D],
                                  mcols[:, j:j + 1], icols[:, j:j + 1],
                                  Alu.subtract, Alu.mult),
                     wait=((f"xl{ci % 2}", V(f"xl{ci % 2}")), ("is_v", 1),
                           ("tp_mm", tp_before), ("xnst", xnst_hist[ci - 1] if ci > 0 else 0)),
                     inc="xn_v", ss=True)
                P.do("scalar", mk("dma_start", out=xn_h[128 * j:128 * j + 128, :],
                                  in_=xn_sb[:, jj * D:(jj + 1) * D]),
                     wait=(("xn_v", V("xn_v")),), inc="xnst", incn=16)
            xnst_hist[ci] = V("xnst")
            lg_ev_prev = V("lg_ev")
            lg_mm_prev = V("lg_mm")
            for dc in range(DC):
                hb = dc % 2
                bank = p01[:, hb * 512: hb * 512 + 512]
                for jj in range(4):
                    P.do("tensor", mk("transpose", bank[:, jj * 128:(jj + 1) * 128],
                                      xn_sb[:, jj * D + dc * 128: jj * D + dc * 128 + 128],
                                      ident[:, :]),
                         wait=(("xn_v", 4 * ci + 4), ("id_g", 1),
                               ("te_v", max(0, V("te_v") - 1)),
                               ("ex_sc", V("ex_sc") if dc <= 1 else 0),
                               ),
                         inc="tp_mm" if jj == 3 else None)
                P.do("vector", mk("tensor_copy", xnT_sb[:, dc * 512:(dc + 1) * 512], bank),
                     wait=(("tp_mm", V("tp_mm")), ("lg_mm", lg_mm_prev)),
                     inc="te_v", ss=True)
            te_done = V("te_v")
            fts = (list(range(2 * DC)) if own else list(range(DC, 2 * DC)))
            for gi, ft in enumerate(fts):
                hb = gi % 2
                bank = p23[:, hb * 512: hb * 512 + 512]
                for dc in range(DC):
                    P.do("tensor", mk("matmul", bank,
                                      wq_sb[:, dc * F + ft * 128: dc * F + ft * 128 + 128],
                                      xnT_sb[:, dc * 512:(dc + 1) * 512],
                                      start=(dc == 0), stop=(dc == DC - 1)),
                         wait=(("te_v", te_done), ("w_ld", 16), ("c_ld", 64),
                               ("qe_v", max(0, V("qe_v") - 1))),
                         inc="qk_mm" if dc == DC - 1 else None)
                g = len(qst_hist)
                eb = qke_sb[gi % 2]
                P.do("vector", mk("tensor_scalar", eb[:, :], bank, bqk_sb[:, ft:ft + 1],
                                  None, Alu.add),
                     wait=(("qk_mm", V("qk_mm")),
                           ((f"qs{g % 2}"), V(f"qs{g % 2}"))),
                     inc="qe_v", ss=True)
                if ft < DC:
                    dst = qT_h[ft, :, 512 * ci: 512 * ci + 512]
                else:
                    dst = kT_h[ft - DC, :, 512 * ci: 512 * ci + 512]
                P.do("scalar", mk("dma_start", out=dst, in_=eb[:, :]),
                     wait=(("qe_v", V("qe_v")),), inc=f"qs{g % 2}", incn=16)
                qst_hist.append(0)
            for jj in range(4):
                j = 4 * ci + jj
                vt = 4 * ci + jj
                vp = [p45, p67][vt % 2]
                for dc in range(DC):
                    for hv in range(NHV):
                        P.do("tensor", mk("matmul", vp[:, hv * 512:hv * 512 + NW],
                                          xnT_sb[:, dc * 512 + jj * 128: dc * 512 + jj * 128 + 128],
                                          wq_sb[:, dc * F + 2 * D + hv * 512: dc * F + 2 * D + hv * 512 + NW],
                                          start=(dc == 0), stop=False),
                             wait=(("ve_v", max(0, V("ve_v") - 1)),))
                for hv in range(NHV):
                    P.do("tensor", mk("matmul", vp[:, hv * 512:hv * 512 + NW],
                                      onesc[:, :], bv_sb[:, hv * 512:hv * 512 + NW],
                                      start=False, stop=True),
                         inc="v_mm" if hv == NHV - 1 else None)
                g = len(vst_hist)
                vbuf = ve_sb[jj % 2]
                P.do("vector", mk("tensor_copy", vbuf[:, 0:D], vp[:, 0:D]),
                     wait=(("v_mm", V("v_mm")),
                           ((f"vs{jj % 2}"), V(f"vs{jj % 2}"))),
                     inc="ve_v", ss=True)
                P.do("scalar", mk("dma_start", out=v_h[128 * j:128 * j + 128, :], in_=vbuf[:, 0:D]),
                     wait=(("ve_v", V("ve_v")),), inc=f"vs{jj % 2}", incn=16)
                vst_hist.append(0)
                lg = p01[:, jj * E: jj * E + E]
                for dc in range(DC):
                    P.do("tensor", mk("matmul", lg,
                                      xnT_sb[:, dc * 512 + jj * 128: dc * 512 + jj * 128 + 128],
                                      gw_sb[:, dc * E:(dc + 1) * E],
                                      start=(dc == 0), stop=(dc == DC - 1)),
                         wait=(("te_v", te_done), ("ex_sc", V("ex_sc")),),
                         inc="lg_mm" if dc == DC - 1 else None)
                P.do("vector", mk("reduce_max", tmp1[:, 0:1], lg, axis=AX.X),
                     wait=(("lg_mm", V("lg_mm")),), ss=True)
                P.do("vector", mk("tensor_scalar", tmp1[:, 0:1], tmp1[:, 0:1], -1.0, None, Alu.mult),
                     inc="mx_v", ss=True)
                P.do("scalar", mk("activation", afc_sb[:, jj * E:(jj + 1) * E], lg, Act.Exp,
                                  bias=tmp1[:, 0:1], scale=1.0, accum_out=tmp2[:, 0:1]),
                     wait=(("mx_v", V("mx_v")),), inc="ex_sc")
                P.do("vector", mk("reciprocal", tmp2[:, 1:2], tmp2[:, 0:1]),
                     wait=(("ex_sc", V("ex_sc")),), inc="vself")
                P.wait("vector", "vself", V("vself"))
                P.do("vector", mk("tensor_scalar", afc_sb[:, jj * E:(jj + 1) * E],
                                  afc_sb[:, jj * E:(jj + 1) * E], tmp2[:, 1:2], None, Alu.mult), ss=True)
                P.do("vector", mk("tensor_copy",
                                  aff_tm[:, j: j + (E - 1) * JF + 1: JF],
                                  afc_sb[:, jj * E:(jj + 1) * E]))
                P.do("vector", mk("tensor_copy", afp_sb[:, jj * 64: jj * 64 + E],
                                  afc_sb[:, jj * E:(jj + 1) * E]),
                     wait=(("afst", V("afst") if False else 0),), inc="lg_ev")
            P.do("scalar", mk("dma_start",
                              out=af_h[512 * ci:512 * ci + 512, :].rearrange("(a p) f -> p a f", p=128),
                              in_=afp_sb[:, :].rearrange("p (a f) -> p a f", f=64)),
                 wait=(("lg_ev", V("lg_ev")),), inc="afst", incn=16)

    p1_xnst = V("xnst")
    p1_qs0, p1_qs1 = V("qs0"), V("qs1")
    p1_vs0, p1_vs1 = V("vs0"), V("vs1")
    p1_afst = V("afst")
    p1_lgmm = V("lg_mm")
    p1_vmm = V("v_mm")
    p1_exsc = V("ex_sc")
    aff_done = V("lg_ev")
    if pend == 0:
        P.wait("vector", "st_v", V("st_v"))
        P.wait("scalar", "st_s", V("st_s"))
    if pend == 1:
        P.wait("scalar", "xnst", V("xnst"))
        P.wait("scalar", "qs0", V("qs0"))
        P.wait("scalar", "qs1", V("qs1"))
        P.wait("scalar", "vs0", V("vs0"))
        P.wait("scalar", "vs1", V("vs1"))
        P.wait("scalar", "afst", V("afst"))

    if pend >= 2:
        aff3 = aff_tm.rearrange("p (e j) -> p e j", e=E)
        msk3 = msk.rearrange("p (e j) -> p e j", e=E)
        P.do("gpsimd", mk("load_library", libcfg.attn))
        P.do("vector", mk("memset", lo[:, :], 0.0), wait=(("lg_ev", aff_done),))
        P.do("vector", mk("memset", hi[:, :], 1.0))
        for it in range(cfg.NITER):
            if it > 0:
                P.do("vector", mk("tensor_scalar", pred[:, :], cnt[:, :], float(cfg.KSEL), None, Alu.is_ge),
                     wait=(("rt_gv", it),), ss=True)
                P.do("vector", mk("tensor_tensor", dd1[:, :], mid[:, :], lo[:, :], Alu.subtract), ss=True)
                P.do("vector", mk("tensor_tensor", dd1[:, :], pred[:, :], dd1[:, :], Alu.mult), ss=True)
                P.do("vector", mk("tensor_tensor", lo[:, :], lo[:, :], dd1[:, :], Alu.add), ss=True)
                P.do("vector", mk("tensor_tensor", dd1[:, :], hi[:, :], mid[:, :], Alu.subtract), ss=True)
                P.do("vector", mk("tensor_tensor", dd1[:, :], pred[:, :], dd1[:, :], Alu.mult), ss=True)
                P.do("vector", mk("tensor_tensor", hi[:, :], mid[:, :], dd1[:, :], Alu.add), ss=True)
            P.do("vector", mk("tensor_tensor", mid[:, :], lo[:, :], hi[:, :], Alu.add), ss=True)
            P.do("vector", mk("tensor_scalar", mid[:, :], mid[:, :], 0.5, None, Alu.mult), ss=True)
            P.do("vector", mk("tensor_tensor", msk3[:, :, :], aff3[:, :, :],
                              mid.unsqueeze(2).broadcast_to([128, E, JF]), Alu.is_ge), ss=True)
            P.do("vector", mk("tensor_reduce", cnt[:, :], msk3[:, :, :], op=Alu.add, axis=AX.X),
                 inc="rt_vg")
            P.do("gpsimd", mk("partition_all_reduce", cnt[:, :], cnt[:, :], 128, ReduceOp.add),
                 wait=(("rt_vg", it + 1), ("id_g", 2)), inc="rt_gv")
        P.do("vector", mk("tensor_scalar", pred[:, :], cnt[:, :], float(cfg.KSEL), None, Alu.is_ge),
             wait=(("rt_gv", cfg.NITER),), ss=True)
        P.do("vector", mk("tensor_tensor", dd1[:, :], mid[:, :], lo[:, :], Alu.subtract), ss=True)
        P.do("vector", mk("tensor_tensor", dd1[:, :], pred[:, :], dd1[:, :], Alu.mult), ss=True)
        P.do("vector", mk("tensor_tensor", lo[:, :], lo[:, :], dd1[:, :], Alu.add), inc="lo_v", ss=True)
        affo3 = aff_tm.rearrange("p (e j) -> p e j", e=E)[:, :, 0:JH]
        mif3 = mif.rearrange("p (e j) -> p e j", e=E)
        P.do("vector", mk("tensor_tensor", mif3[:, :, :], affo3[:, :, :],
                          lo.unsqueeze(2).broadcast_to([128, E, JH]), Alu.is_ge), ss=True)
        P.do("vector", mk("tensor_tensor", mif[:, :], mif[:, :], iota1[:, :], Alu.mult),
             wait=(("id_g", 2),), ss=True)
        P.do("vector", mk("tensor_scalar", mif[:, :], mif[:, :], -1.0, None, Alu.add), ss=True)
        P.do("vector", mk("tensor_copy", mi16[:, :], mif[:, :]), inc="msk_v", ss=True)
        for g in range(8):
            src = mi16[16 * g:16 * g + 16, :].rearrange("p (e j) -> p e j", e=E)
            dst = wrap[0:16, :].rearrange("p (e g j) -> p e g j", e=E, g=8)[:, :, g, :]
            P.do("sync", mk("dma_start", out=dst, in_=src),
                 wait=(("msk_v", V("msk_v")),), inc="fold_st", incn=16)
        P.do("gpsimd", mk("load_library", libcfg.sparse_gather))
        for e in range(E):
            P.do("gpsimd", mk("sparse_gather", idxs[0:16, e * ICOLS:(e + 1) * ICOLS],
                              wrap[:, e * 8 * JH:(e + 1) * 8 * JH],
                              num_found=nfound[0:1, e:e + 1]),
                 wait=(("fold_st", V("fold_st")),), inc="sg_g")
        P.do("sync", mk("dma_start", out=nfound[1:2, :], in_=nfound[0:1, :]),
             wait=(("sg_g", V("sg_g")),), inc="nfb_st", incn=16)
        P.do("sync", mk("dma_start", out=nfound[2:4, :], in_=nfound[0:2, :]),
             wait=(("nfb_st", 16),), inc="nfb_st", incn=16)
        P.do("sync", mk("dma_start", out=nfound[4:8, :], in_=nfound[0:4, :]),
             wait=(("nfb_st", 32),), inc="nfb_st", incn=16)
        P.do("sync", mk("dma_start", out=nfound[8:16, :], in_=nfound[0:8, :]),
             wait=(("nfb_st", 48),), inc="nfb_st", incn=16)
        nf_f = cnt
        P.do("vector", mk("tensor_copy", nf_f[0:16, 0:E], nfound[:, :]),
             wait=(("nfb_st", 64), ("sg_g", V("sg_g")),), ss=True)
        vmq = mif[0:16, 0:ICOLS]
        ivf = mif[0:16, ICOLS:2 * ICOLS]
        for e in range(E):
            P.do("vector", mk("tensor_scalar", vmq, sio[:, :], nf_f[0:16, e:e + 1],
                              None, Alu.is_lt),
                 wait=(("id_g", 3),), ss=True)
            P.do("vector", mk("tensor_copy", ivf, idxs[0:16, e * ICOLS:(e + 1) * ICOLS]),
                 ss=True)
            P.do("vector", mk("tensor_scalar", ivf, ivf, 1.0, None, Alu.add), ss=True)
            P.do("vector", mk("tensor_tensor", ivf, ivf, vmq, Alu.mult), ss=True)
            P.do("vector", mk("tensor_scalar", ivf, ivf, -1.0, None, Alu.add), ss=True)
            P.do("vector", mk("tensor_copy", idxs[0:16, e * ICOLS:(e + 1) * ICOLS], ivf),
                 inc="cln_v", ss=True)
        P.do("vector", mk("tensor_scalar", gidx[0:16, :], idxs[0:16, :], 0, 0, Alu.max, Alu.add),
             wait=(("cln_v", V("cln_v")),), ss=True)
        P.do("vector", mk("tensor_scalar", smsk[0:16, :], idxs[0:16, :], 0, SO + 1, Alu.is_lt, Alu.mult), ss=True)
        P.do("vector", mk("tensor_tensor", sidx[0:16, :], idxs[0:16, :], smsk[0:16, :], Alu.add),
             inc="idx_v", ss=True)
        P.do("gpsimd", mk("dma_start", out=gidx[16:32, :], in_=gidx[0:16, :]),
             wait=(("idx_v", 1),), inc="rep_st", incn=16)
        P.do("gpsimd", mk("dma_start", out=sidx[16:32, :], in_=sidx[0:16, :]),
             inc="rep_st", incn=16)
        P.do("gpsimd", mk("load_library", libcfg.mlp))
        if debug:
            P.do("gpsimd", mk("dma_start", out=dbg_bi[:, 0:E], in_=cnt[:, :]), inc="dbg_g", incn=16)
            P.do("gpsimd", mk("dma_start", out=dbg_bi[:, E:2 * E], in_=hi[:, :]), inc="dbg_g", incn=16)
            P.do("gpsimd", mk("dma_start", out=dbg_bi[:, 2 * E:3 * E], in_=msk[:, 0:E]), inc="dbg_g", incn=16)
            P.do("gpsimd", mk("dma_start", out=dbg_aff[:, :], in_=aff_tm[:, :]), inc="dbg_g", incn=16)
            P.do("gpsimd", mk("dma_start", out=dbg_lo[:, :], in_=lo[:, :]),
                 wait=(("lo_v", 1),), inc="dbg_g", incn=16)
            P.do("gpsimd", mk("dma_start", out=dbg_nf[:, :], in_=nfound[0:1, :]), inc="dbg_g", incn=16)
            P.do("gpsimd", mk("dma_start", out=dbg_idx[:, :], in_=idxs[:, :]), inc="dbg_g", incn=16)


    if pend >= 3:
        P.do("sync", mk("dma_start", out=wo_sb[:, :].rearrange("p (dc f) -> p dc f", dc=DC),
                        in_=wout[:, :].rearrange("(dc p) f -> p dc f", p=128)),
             wait=(("lg_mm", p1_lgmm), ("v_mm", p1_vmm), ("ex_sc", p1_exsc),
                   ("lo_v", 1)),
             inc="w3_ld", incn=16)
        ctx_banks = [p01, p23, p45]
        kt_hist = []
        vbl_hist = []
        ou_slot_hist = {}
        q0 = 0
        for qi, nqt in enumerate(cfg.QCS):
            WQ = 128 * nqt
            P.do("sync", mk("dma_start", out=qt_sb[:, 0:DC * WQ].rearrange("p (dc c) -> p dc c", dc=DC),
                            in_=qT_h[:, :, q0:q0 + WQ].rearrange("dc p c -> p dc c")),
                 wait=(("qs0", p1_qs0), ("qs1", p1_qs1), ("xnst", p1_xnst),
                       ("vs0", p1_vs0), ("vs1", p1_vs1),
                       ("op_mm", V("op_mm"))),
                 inc="q_ld", incn=16)
            q_ld_here = V("q_ld")
            for kb in range(NKB):
                gkb = len(kt_hist)
                pb = gkb % 2
                P.do("sync", mk("dma_start",
                                out=kt_sb[pb][:, :].rearrange("p (dc c) -> p dc c", dc=DC),
                                in_=kT_h[:, :, KB * kb:KB * kb + KB].rearrange("dc p c -> p dc c")),
                     wait=(("ete_v", kt_hist[gkb - 2] if gkb >= 2 else 0),),
                     inc=f"ktl{pb}", incn=16)
                P.do("sync", mk("dma_start",
                                out=vb_sb[pb][:, :].rearrange("p (a f) -> p a f", a=NKC),
                                in_=v_h[KB * kb:KB * kb + KB, :].rearrange("(a p) f -> p a f", p=128)),
                     wait=(("cx_mm", vbl_hist[gkb - 2] if gkb >= 2 else 0),),
                     inc=f"vbl{pb}", incn=16)
                kt_here = V(f"ktl{pb}")
                vb_here = V(f"vbl{pb}")
                cte_prev = V("cte_v")
                for qt in range(nqt):
                    for dc in range(DC):
                        P.do("tensor", mk("matmul", p67[:, 0:KB],
                                          qt_sb[:, dc * WQ + qt * 128: dc * WQ + qt * 128 + 128],
                                          kt_sb[pb][:, dc * KB:(dc + 1) * KB],
                                          start=(dc == 0), stop=(dc == DC - 1)),
                             wait=((f"ktl{pb}", kt_here), ("q_ld", q_ld_here),
                                   ("ex3_sc", V("ex3_sc")), ("cte_v", cte_prev)),
                             inc="s_mm" if dc == DC - 1 else None)
                    P.do("scalar", mk("activation", E_sb[:, qt * KB:(qt + 1) * KB],
                                      p67[:, 0:KB], Act.Exp, bias=shf_sb[:, 0:1],
                                      scale=1.0 / math.sqrt(D),
                                      accum_out=Lcol[:, qt * 16 + kb: qt * 16 + kb + 1]),
                         wait=(("s_mm", V("s_mm")), ("etm_mm", V("etm_mm")), ("c_v", 5)),
                         inc="ex3_sc", ss=True)
                    if debug and qi == 0 and kb == 0 and qt == 0:
                        P.do("scalar", mk("dma_start", out=dbg_SE[1, :, :],
                                          in_=E_sb[:, 0:KB]),
                             wait=(("ex3_sc", V("ex3_sc")),), inc="dbg_st", incn=16)
                ex_here = V("ex3_sc")
                for kc in range(NKC):
                    for qt in range(nqt):
                        P.do("tensor", mk("transpose",
                                          p67[:, 512 + qt * 128: 512 + qt * 128 + 128],
                                          E_sb[:, qt * KB + kc * 128: qt * KB + kc * 128 + 128],
                                          ident[:, :]),
                             wait=(("ex3_sc", ex_here), ("ete_v", V("ete_v")),),
                             inc="etm_mm" if qt == nqt - 1 else None)
                    P.do("vector", mk("tensor_copy", et_sb[:, kc * 384: kc * 384 + WQ],
                                      p67[:, 512:512 + WQ]),
                         wait=(("etm_mm", V("etm_mm")), ("cx_mm", V("cx_mm"))),
                         inc="ete_v", ss=True)
                    ete_here = V("ete_v")
                    for qt in range(nqt):
                        for hv in range(NHV):
                            P.do("tensor", mk("matmul",
                                              ctx_banks[qt][:, hv * 512: hv * 512 + NW],
                                              et_sb[:, kc * 384 + qt * 128: kc * 384 + qt * 128 + 128],
                                              vb_sb[pb][:, kc * D + hv * 512: kc * D + hv * 512 + NW],
                                              start=(kb == 0 and kc == 0),
                                              stop=(kb == NKB - 1 and kc == NKC - 1)),
                                 wait=(("ete_v", ete_here), (f"vbl{pb}", vb_here),
                                   ("oev_v", V("oev_v"))),
                                 inc="cx_mm" if (qt == nqt - 1 and hv == NHV - 1) else None)
                kt_hist.append(V("ete_v"))
                vbl_hist.append(V("cx_mm"))
            cx_done = V("cx_mm")
            for qt in range(nqt):
                P.do("vector", mk("reduce_sum", ell[:, qt:qt + 1],
                                  Lcol[:, qt * 16: qt * 16 + NKB], axis=AX.X),
                     wait=(("cx_mm", cx_done),), ss=True)
                P.do("vector", mk("reciprocal", ell[:, qt:qt + 1], ell[:, qt:qt + 1]),
                     inc="vself")
                P.wait("vector", "vself", V("vself"))
                P.do("vector", mk("tensor_scalar", cx_sb[:, qt * D:(qt + 1) * D],
                                  ctx_banks[qt][:, 0:D], ell[:, qt:qt + 1], None, Alu.mult),
                     inc="cb_v", ss=True)
            cb_here = V("cb_v")
            if debug and qi < 2:
                P.do("scalar", mk("dma_start", out=dbg_cx[qi, :, :], in_=cx_sb[:, 0:3 * D]),
                     wait=(("cb_v", V("cb_v")),), inc="dbg_st", incn=16)
                P.do("scalar", mk("dma_start", out=dbg_el[qi, :, 0:48],
                                  in_=Lcol[:, 0:48]), inc="dbg_st", incn=16)
                P.do("scalar", mk("dma_start", out=dbg_el[qi, :, 48:51],
                                  in_=ell[:, 0:3]), inc="dbg_st", incn=16)
            for dc in range(DC):
                hb = dc % 2
                bank = p67[:, hb * 512: hb * 512 + 512]
                for qt in range(nqt):
                    P.do("tensor", mk("transpose", bank[:, qt * 128:(qt + 1) * 128],
                                      cx_sb[:, qt * D + dc * 128: qt * D + dc * 128 + 128],
                                      ident[:, :]),
                         wait=(("cb_v", cb_here), ("cte_v", max(0, V("cte_v") - 1)),),
                         inc="ctm_mm" if qt == nqt - 1 else None)
                P.do("vector", mk("tensor_copy", cxT_sb[:, dc * 384: dc * 384 + WQ],
                                  bank[:, 0:WQ]),
                     wait=(("ctm_mm", V("ctm_mm")),), inc="cte_v", ss=True)
            cte_here = V("cte_v")
            P.do("sync", mk("dma_start",
                            out=xr_sb[:, 0:nqt * D].rearrange("p (a f) -> p a f", a=nqt),
                            in_=xf[q0:q0 + WQ, :].rearrange("(a p) f -> p a f", p=128)),
                 wait=(("oev_v", V("oev_v")),),
                 inc="res_ld", incn=16)
            res_here = V("res_ld")
            for qt in range(nqt):
                bank = ctx_banks[qt]
                for hv in range(NHV):
                    for dc in range(DC):
                        P.do("tensor", mk("matmul", bank[:, hv * 512: hv * 512 + NW],
                                          cxT_sb[:, dc * 384 + qt * 128: dc * 384 + qt * 128 + 128],
                                          wo_sb[:, dc * D + hv * 512: dc * D + hv * 512 + NW],
                                          start=(dc == 0), stop=False),
                             wait=(("cte_v", cte_here), ("w3_ld", 16),))
                    P.do("tensor", mk("matmul", bank[:, hv * 512: hv * 512 + NW],
                                      onesc[:, :], bo_sb[:, hv * 512: hv * 512 + NW],
                                      start=False, stop=True),
                         inc="op_mm" if hv == NHV - 1 else None)
                P.do("vector", mk("tensor_tensor", ou_sb[:, qt * D:(qt + 1) * D],
                                  bank[:, 0:D], xr_sb[:, qt * D:(qt + 1) * D], Alu.add),
                     wait=(("op_mm", V("op_mm")), ("res_ld", res_here),
                           ("out_st", V("out_st"))),
                     inc="oev_v", ss=True)
                P.do("scalar", mk("dma_start", out=out[q0 + qt * 128: q0 + qt * 128 + 128, :],
                                  in_=ou_sb[:, qt * D:(qt + 1) * D]),
                     wait=(("oev_v", V("oev_v")),), inc="out_st", incn=16)
                ou_slot_hist[qt] = V("out_st")
            q0 += WQ
        out_rows_done = V("out_st")
        p3_opmm = V("op_mm")
        p3_oev = V("oev_v")
        p3_cte = V("cte_v")


    if pend >= 4:
        RREG = {}

        def _mkreg(e):
            RREG["t"] = e.to_reg(TCH)
            return None
        P.do("gpsimd", _mkreg)

        def mk_gather(dst, srch, icol0):
            def fn(e):
                return e.dma_gather(dst, srch, gidx[:, icol0:icol0 + ICC], TCH, RREG["t"],
                                    dst.shape[-1])
            return fn

        def mk_scatter(icol0):
            def fn(e):
                return e.dma_scatter_add(out[:, :], y_sb[:, :].rearrange("p (a f) -> p a f", f=D),
                                         sidx[:, icol0:icol0 + ICC], TCH, RREG["t"], D)
            return fn
        SNUM = [0]
        STRIP_H = []
        gt_hist = []
        ga_hist = []
        gu_hist = []
        dn_hist = []
        for e in range(E):
            for c in range(NTCH):
                ch = e * NTCH + c
                gpar = ch % 2
                icol0 = e * ICOLS + c * ICC
                gu_hist.append([])
                dn_hist.append([])
                P.do("gpsimd", mk_gather(gb_sb[gpar][:, :].rearrange("p (a f) -> p a f", f=D),
                                         xn_h[:, :], icol0),
                     wait=(("rep_st", 32), ("xnst", p1_xnst), ("afst", p1_afst),
                           ("op_mm", p3_opmm), ("oev_v", p3_oev),
                           ("gt_mm", gt_hist[ch - 2] if ch >= 2 else 0),
                           ("ga_v", ga_hist[ch - 2] if ch >= 2 else 0)),
                     inc=f"gd{gpar}", incn=16)
                P.do("gpsimd", mk_gather(ab_sb[gpar][:, :].rearrange("p (a f) -> p a f", f=64),
                                         af_h[:, :], icol0),
                     inc=f"gd{gpar}", incn=16)
                gds = f"gd{gpar}"
                g_here = V(gds)
                P.do("vector", mk("tensor_copy", ga_sb[:, 0:TPC],
                                  ab_sb[gpar][:, e: e + (TPC - 1) * 64 + 1: 64] if TPC > 1
                                  else ab_sb[gpar][:, e:e + 1]),
                     wait=((gds, g_here), ("ysc_v", V("ysc_v")),), inc="ga_v")
                ga_hist.append(V("ga_v"))
                gu_prev_chunk = V("gu_mm")
                for dc in range(DC):
                    hb = dc % 2
                    bank = p45[:, hb * 512: hb * 512 + 512]
                    for tt in range(TPC):
                        P.do("tensor", mk("transpose", bank[:, tt * 128:(tt + 1) * 128],
                                          gb_sb[gpar][:, tt * D + dc * 128: tt * D + dc * 128 + 128],
                                          ident[:, :]),
                             wait=((gds, g_here),
                                   ("gte_v", max(0, V("gte_v") - 1)),
                                   ("ysc_v", V("ysc_v")),),
                             inc="gt_mm" if tt == TPC - 1 else None)
                    xs_ = slice(dc * TCH, (dc + 1) * TCH)
                    P.do("vector", mk("tensor_copy", xgh_sb[:, xs_], bank[:, 0:TCH]),
                         wait=(("gt_mm", V("gt_mm")), ("gu_mm", gu_prev_chunk)),
                         ss=True)
                    P.do("vector", mk("tensor_copy", wscr[:, 0:TCH], xgh_sb[:, xs_]),
                         ss=True)
                    P.do("vector", mk("tensor_tensor", wscr[:, 0:TCH], bank[:, 0:TCH],
                                      wscr[:, 0:TCH], Alu.subtract), ss=True)
                    P.do("vector", mk("tensor_copy", xgl_sb[:, xs_], wscr[:, 0:TCH]),
                         inc="gte_v", ss=True)
                gt_hist.append(V("gt_mm"))
                gte_here = V("gte_v")
                xsp_here = gte_here
                dn_prev_chunk = V("dn_mm")
                for it in range(IC):
                    wpar = it % 2
                    guard = (gu_hist[ch][it - 2] if it >= 2 else
                             (gu_hist[ch - 1][IC - 2 + it] if ch >= 1 else 0))
                    SW = 512 if (DC * 128) % 512 == 0 else DC * 128
                    NPS = SW // 128
                    NGS = (DC * 128) // SW
                    for srcT, hb_, lb_ in ((wgT, wgh_sb[wpar], wgl_sb[wpar]),
                                           (wuT, wuh_sb[wpar], wul_sb[wpar])):
                        for st in range(NGS):
                            sp = SNUM[0] % 2
                            wsv = wstrip[sp][:, 0:SW]
                            P.do("sync", mk("dma_start",
                                            out=wsv.rearrange(
                                                "p (dc i) -> p dc i", dc=NPS),
                                            in_=srcT[e, it, :, st * NPS:(st + 1) * NPS, :]),
                                 wait=(("wsp_v", STRIP_H[SNUM[0] - 2] if SNUM[0] >= 2 else 0),
                                       ("gu_mm", guard),
                                       ("oev_v", p3_oev), ("cte_v", p3_cte)),
                                 inc=f"wt{sp}", incn=16)
                            ws_ = slice(st * SW, (st + 1) * SW)
                            P.do("vector", mk("tensor_copy", hb_[:, ws_], wsv),
                                 wait=((f"wt{sp}", V(f"wt{sp}")), ("gu_mm", guard)), ss=True)
                            P.do("vector", mk("tensor_copy", wscr[:, 0:SW], hb_[:, ws_]),
                                 ss=True)
                            P.do("vector", mk("tensor_tensor", wscr[:, 0:SW],
                                              wsv, wscr[:, 0:SW],
                                              Alu.subtract), ss=True)
                            P.do("vector", mk("tensor_copy", lb_[:, ws_], wscr[:, 0:SW]),
                                 inc="wsp_v", ss=True)
                            STRIP_H.append(V("wsp_v"))
                            SNUM[0] += 1
                    wsp_here = V("wsp_v")
                    gu_bank = [p01, p23][it % 2]
                    for p3p, (wl_, xl_) in enumerate(((wgh_sb[wpar], xgh_sb),
                                                      (wgl_sb[wpar], xgh_sb),
                                                      (wgh_sb[wpar], xgl_sb))):
                        for dc in range(DC):
                            P.do("tensor", mk("matmul", gu_bank[:, 0:TCH],
                                              wl_[:, dc * 128:(dc + 1) * 128],
                                              xl_[:, dc * TCH:(dc + 1) * TCH],
                                              start=(p3p == 0 and dc == 0),
                                              stop=(p3p == 2 and dc == DC - 1)),
                                 wait=(("gte_v", xsp_here), ("wsp_v", wsp_here),
                                       ("h_v", max(0, V("h_v") - 1)),),
                                 inc="gu_mm" if (p3p == 2 and dc == DC - 1) else None)
                    for p3p, (wl_, xl_) in enumerate(((wuh_sb[wpar], xgh_sb),
                                                      (wul_sb[wpar], xgh_sb),
                                                      (wuh_sb[wpar], xgl_sb))):
                        for dc in range(DC):
                            P.do("tensor", mk("matmul", gu_bank[:, 512:512 + TCH],
                                              wl_[:, dc * 128:(dc + 1) * 128],
                                              xl_[:, dc * TCH:(dc + 1) * TCH],
                                              start=(p3p == 0 and dc == 0),
                                              stop=(p3p == 2 and dc == DC - 1)),
                                 inc="gu_mm" if (p3p == 2 and dc == DC - 1) else None)
                    gu_hist[ch].append(V("gu_mm"))
                    P.do("scalar", mk("activation", h1_sb[:, 0:TCH],
                                      gu_bank[:, 0:TCH], Act.Silu, bias=0.0, scale=1.0),
                         wait=(("gu_mm", V("gu_mm")), ("dn_mm", dn_prev_chunk),
                               ("hsp_v", V("hsp_v"))),
                         inc="h_sc", ss=True)
                    P.do("vector", mk("tensor_tensor", h1_sb[:, 0:TCH],
                                      h1_sb[:, 0:TCH],
                                      gu_bank[:, 512:512 + TCH], Alu.mult),
                         wait=(("h_sc", V("h_sc")),), inc="h_v", ss=True)
                    hs_ = slice(it * TCH, (it + 1) * TCH)
                    P.do("vector", mk("tensor_copy", hh_sb[:, hs_], h1_sb[:, 0:TCH]), ss=True)
                    P.do("vector", mk("tensor_copy", wscr[:, 0:TCH], hh_sb[:, hs_]), ss=True)
                    P.do("vector", mk("tensor_tensor", wscr[:, 0:TCH], h1_sb[:, 0:TCH],
                                      wscr[:, 0:TCH], Alu.subtract), ss=True)
                    P.do("vector", mk("tensor_copy", hl_sb[:, hs_], wscr[:, 0:TCH]),
                         inc="hsp_v", ss=True)
                h_here = V("hsp_v")
                yt_prev_chunk = V("yt_mm")
                for dt in range(DC):
                    wpar = dt % 2
                    guard_d = (dn_hist[ch][dt - 2] if dt >= 2 else
                               (dn_hist[ch - 1][DC - 2 + dt] if ch >= 1 else 0))
                    SWD = 512 if (IC * 128) % 512 == 0 else IC * 128
                    NPSD = SWD // 128
                    NDS = (IC * 128) // SWD
                    for st in range(NDS):
                        sp = SNUM[0] % 2
                        P.do("sync", mk("dma_start",
                                        out=wstrip[sp][:, 0:SWD].rearrange(
                                            "p (ic d) -> p ic d", ic=NPSD),
                                        in_=wdT[e, dt, :, st * NPSD:(st + 1) * NPSD, :]),
                             wait=(("wsp_v", STRIP_H[SNUM[0] - 2] if SNUM[0] >= 2 else 0),
                                   ("dn_mm", guard_d),
                                   ("oev_v", p3_oev), ("cte_v", p3_cte)),
                             inc=f"wt{sp}", incn=16)
                        ws_ = slice(st * SWD, (st + 1) * SWD)
                        P.do("vector", mk("tensor_copy", wdh_sb[wpar][:, ws_],
                                          wstrip[sp][:, 0:SWD]),
                             wait=((f"wt{sp}", V(f"wt{sp}")), ("dn_mm", guard_d)), ss=True)
                        P.do("vector", mk("tensor_copy", wscr[:, 0:SWD],
                                          wdh_sb[wpar][:, ws_]), ss=True)
                        P.do("vector", mk("tensor_tensor", wscr[:, 0:SWD],
                                          wstrip[sp][:, 0:SWD],
                                          wscr[:, 0:SWD], Alu.subtract), ss=True)
                        P.do("vector", mk("tensor_copy", wdl_sb[wpar][:, ws_],
                                          wscr[:, 0:SWD]),
                             inc="wsp_v", ss=True)
                        STRIP_H.append(V("wsp_v"))
                        SNUM[0] += 1
                    wdsp_here = V("wsp_v")
                    dn_bank = p67[:, (dt % 2) * 512: (dt % 2) * 512 + TCH]
                    for p3p, (wl_, hl_) in enumerate(((wdh_sb[wpar], hh_sb),
                                                      (wdl_sb[wpar], hh_sb),
                                                      (wdh_sb[wpar], hl_sb))):
                        for ic in range(IC):
                            P.do("tensor", mk("matmul", dn_bank,
                                              wl_[:, ic * 128:(ic + 1) * 128],
                                              hl_[:, ic * TCH:(ic + 1) * TCH],
                                              start=(p3p == 0 and ic == 0),
                                              stop=(p3p == 2 and ic == IC - 1)),
                                 wait=(("hsp_v", h_here), ("wsp_v", wdsp_here),
                                       ("ye_v", max(0, V("ye_v") - 1)),),
                                 inc="dn_mm" if (p3p == 2 and ic == IC - 1) else None)
                    dn_hist[ch].append(V("dn_mm"))
                    P.do("vector", mk("tensor_copy", yt_sb[:, dt * TCH:(dt + 1) * TCH], dn_bank),
                         wait=(("dn_mm", V("dn_mm")), ("yt_mm", yt_prev_chunk)),
                         inc="ye_v", ss=True)
                ye_here = V("ye_v")
                sc_prev_chunk = V("sc_dma")
                for tt in range(TPC):
                    for dt in range(DC):
                        P.do("tensor", mk("transpose", p45[:, dt * 128:(dt + 1) * 128],
                                          yt_sb[:, dt * TCH + tt * 128: dt * TCH + tt * 128 + 128],
                                          ident[:, :]),
                             wait=(("ye_v", ye_here), ("ysc_v", V("ysc_v")),),
                             inc="yt_mm" if dt == DC - 1 else None)
                    P.do("vector", mk("tensor_scalar", y_sb[:, tt * D:(tt + 1) * D],
                                      p45[:, 0:D], ga_sb[:, tt:tt + 1], None, Alu.mult),
                         wait=(("yt_mm", V("yt_mm")), ("ga_v", V("ga_v")),
                               ("sc_dma", sc_prev_chunk)),
                         inc="ysc_v", ss=True)
                ysc_here = V("ysc_v")
                P.do("gpsimd", mk_scatter(icol0),
                     wait=(("ysc_v", ysc_here), ("out_st", out_rows_done),),
                     inc="sc_dma", incn=16)

    if pend >= 4:
        P.wait("gpsimd", "sc_dma", V("sc_dma"))
        P.wait("scalar", "out_st", V("out_st"))
    if pend == 2:
        P.wait("gpsimd", "rep_st", V("rep_st"))
    if pend == 3:
        P.wait("scalar", "out_st", V("out_st"))
    if debug:
        P.wait("gpsimd", "dbg_g", V("dbg_g"))

    sem_names = []
    for e2 in ENGINES:
        for w, _, post in P.items[e2]:
            for s, _v in w:
                if s not in sem_names:
                    sem_names.append(s)
            for ps, _pn in post:
                if ps not in sem_names:
                    sem_names.append(ps)
    import contextlib
    stack = contextlib.ExitStack()
    sems = {s: stack.enter_context(nc.semaphore(s)) for s in sem_names}

    def body(eng_name):
        def run(eng):
            for w, fn, post in P.items[eng_name]:
                for s, v in w:
                    eng.wait_ge(sems[s], v)
                if fn is None:
                    continue
                inst = fn(eng)
                for ps, pn in post:
                    inst.then_inc(sems[ps], pn)
        return run

    with nc.Block() as blk:
        blk.sync(body("sync"))
        blk.vector(body("vector"))
        blk.scalar(body("scalar"))
        blk.tensor(body("tensor"))
        blk.gpsimd(body("gpsimd"))
    stack.close()
    return nc


def _stage_inputs(inputs):
    x = np.ascontiguousarray(np.asarray(inputs["L0_attn_x"], dtype=np.float32))
    wqkv = np.ascontiguousarray(np.asarray(inputs["L0_attn_W_qkv"], dtype=np.float32))
    bqkv = np.ascontiguousarray(np.asarray(inputs["L0_attn_b_qkv"], dtype=np.float32))
    wout = np.ascontiguousarray(np.asarray(inputs["L0_attn_W_out"], dtype=np.float32))
    bout = np.ascontiguousarray(np.asarray(inputs["L0_attn_b_out"], dtype=np.float32))
    gw = np.asarray(inputs["L0_ffn_gate_w"], dtype=np.float32)
    wg = np.asarray(inputs["L0_ffn_w_gate"], dtype=np.float32)
    wu = np.asarray(inputs["L0_ffn_w_up"], dtype=np.float32)
    wd = np.asarray(inputs["L0_ffn_w_down"], dtype=np.float32)
    E, I, D = wg.shape
    gwT = np.ascontiguousarray(gw.T)
    IC, DC = I // 128, D // 128
    wgT = np.ascontiguousarray(wg.reshape(E, IC, 128, DC, 128).transpose(0, 1, 4, 3, 2))
    wuT = np.ascontiguousarray(wu.reshape(E, IC, 128, DC, 128).transpose(0, 1, 4, 3, 2))
    wdT = np.ascontiguousarray(wd.reshape(E, DC, 128, IC, 128).transpose(0, 1, 4, 3, 2))
    B, S, _ = x.shape
    per_core = []
    for c in range(8):
        b, h = c // 2, c % 2
        own = x[b, (S // 2) * h:(S // 2) * (h + 1)]
        oth = x[b, (S // 2) * (1 - h):(S // 2) * (2 - h)]
        xfc = np.ascontiguousarray(np.concatenate([own, oth], axis=0))
        per_core.append({
            "xf": xfc, "wqkv": wqkv, "bqkv": bqkv, "wout": wout, "bout": bout,
            "gwT": gwT, "wgT": wgT, "wuT": wuT, "wdT": wdT,
        })
    return per_core, (B, S, D)


def kernel(**inputs):
    cfg = Cfg()
    in_maps, (B, S, D) = _stage_inputs(inputs)
    nc = build_program(cfg)
    mybir.codegen_inst_isa_subclasses(nc)
    res = run_bass_kernel_spmd(nc, in_maps, core_ids=list(range(8)))
    outf = np.empty((B, S, D), dtype=np.float32)
    for c in range(8):
        b, h = c // 2, c % 2
        outf[b, (S // 2) * h:(S // 2) * (h + 1)] = res.results[c]["out"][:S // 2]
    return outf


# version 05
# speedup vs baseline: 1.2140x; 1.2140x over previous
import math
import numpy as np

import concourse.bass as bass
import concourse.mybir as mybir
from concourse.bass_utils import run_bass_kernel_spmd
from concourse.bass_isa import ReduceOp
from concourse import library_config as libcfg

F32 = mybir.dt.float32
I16 = mybir.dt.int16
U32 = mybir.dt.uint32
Alu = mybir.AluOpType
AX = mybir.AxisListType
Act = mybir.ActivationFunctionType

ENGINES = ("sync", "vector", "scalar", "tensor", "gpsimd")


class Cfg:
    def __init__(self, D=1024, S=8192, E=8, I=4096, CAP=1152, TCH=384,
                 KB=512, NITER=56, SHIFT=12.0, EPS=1e-5):
        self.D, self.S, self.E, self.I = D, S, E, I
        self.SO = S // 2
        self.CAP, self.TCH = CAP, TCH
        self.KB = KB
        self.NITER = NITER
        self.SHIFT = SHIFT
        self.EPS = EPS
        self.DC = D // 128
        self.IC = I // 128
        self.NT = S // 128
        self.NCH = S // 512
        self.NKB = S // KB
        self.NTCH = CAP // TCH
        self.KSEL = 2 * S // E
        nqt = self.SO // 128
        self.QCS = []
        while nqt > 0:
            t = min(3, nqt)
            self.QCS.append(t)
            nqt -= t


class Prog:

    def __init__(self):
        self.items = {e: [] for e in ENGINES}
        self.counts = {}
        self.last_wait = {e: {} for e in ENGINES}

    def val(self, sem):
        return self.counts.get(sem, 0)

    def do(self, eng, fn, wait=(), inc=None, incn=1, ss=False):
        w = []
        for s, v in wait:
            if v <= 0:
                continue
            if self.last_wait[eng].get(s, -1) >= v:
                continue
            self.last_wait[eng][s] = v
            w.append((s, v))
        post = []
        if inc is not None:
            self.counts[inc] = self.counts.get(inc, 0) + incn
            post.append((inc, incn))
        if ss and inc is None:
            sn = "ss_" + eng
            self.counts[sn] = self.counts.get(sn, 0) + 1
            post.append((sn, 1))
        self.items[eng].append((tuple(w), fn, tuple(post)))
        if ss:
            sw, svv = post[-1][0], self.counts[post[-1][0]]
            self.items[eng].append((((sw, svv),), None, ()))
            self.last_wait[eng][sw] = svv
        return self.counts.get(inc, 0) if inc else None

    def wait(self, eng, sem, v):
        self.do(eng, None, wait=((sem, v),))


def build_program(cfg, debug=False, pend=4):
    nc = bass.Bass("TRN2", target_bir_lowering=False, detect_race_conditions=False)
    D, S, E, I = cfg.D, cfg.S, cfg.E, cfg.I
    SO, CAP, TCH, KB = cfg.SO, cfg.CAP, cfg.TCH, cfg.KB
    DC, IC, NT, NCH, NKB, NTCH = cfg.DC, cfg.IC, cfg.NT, cfg.NCH, cfg.NKB, cfg.NTCH
    F = 3 * D
    JH = SO // 128
    JF = NT
    ICOLS = CAP // 16
    ICC = TCH // 16
    TPC = TCH // 128
    NKC = KB // 128
    NHV = max(1, D // 512)
    NW = min(512, D)

    xf = nc.dram_tensor("xf", [S, D], F32, kind="ExternalInput").ap()
    wqkv = nc.dram_tensor("wqkv", [D, F], F32, kind="ExternalInput").ap()
    bqkv = nc.dram_tensor("bqkv", [F], F32, kind="ExternalInput").ap()
    wout = nc.dram_tensor("wout", [D, D], F32, kind="ExternalInput").ap()
    bout = nc.dram_tensor("bout", [D], F32, kind="ExternalInput").ap()
    gwT = nc.dram_tensor("gwT", [D, E], F32, kind="ExternalInput").ap()
    wgT = nc.dram_tensor("wgT", [E, IC, 128, DC, 128], F32, kind="ExternalInput").ap()
    wuT = nc.dram_tensor("wuT", [E, IC, 128, DC, 128], F32, kind="ExternalInput").ap()
    wdT = nc.dram_tensor("wdT", [E, DC, 128, IC, 128], F32, kind="ExternalInput").ap()
    out = nc.dram_tensor("out", [SO + 128, D], F32, kind="ExternalOutput").ap()

    hbm_kind = "ExternalOutput" if debug else "Internal"
    xn_h = nc.dram_tensor("xn_h", [S, D], F32, kind=hbm_kind).ap()
    af_h = nc.dram_tensor("af_h", [S, 64], F32, kind=hbm_kind).ap()
    qT_h = nc.dram_tensor("qT_h", [DC, 128, SO], F32, kind=hbm_kind).ap()
    kT_h = nc.dram_tensor("kT_h", [DC, 128, S], F32, kind=hbm_kind).ap()
    v_h = nc.dram_tensor("v_h", [S, D], F32, kind=hbm_kind).ap()
    if debug:
        dbg_aff = nc.dram_tensor("dbg_aff", [128, E * JF], F32, kind="ExternalOutput").ap()
        dbg_lo = nc.dram_tensor("dbg_lo", [128, E], F32, kind="ExternalOutput").ap()
        dbg_nf = nc.dram_tensor("dbg_nf", [1, E], U32, kind="ExternalOutput").ap()
        dbg_idx = nc.dram_tensor("dbg_idx", [128, E * ICOLS], I16, kind="ExternalOutput").ap()
        dbg_st1 = nc.dram_tensor("dbg_st1", [128, 5 * NT], F32, kind="ExternalOutput").ap()
        dbg_bi = nc.dram_tensor("dbg_bi", [128, 3 * E], F32, kind="ExternalOutput").ap()
        dbg_cx = nc.dram_tensor("dbg_cx", [2, 128, 3 * D], F32, kind="ExternalOutput").ap()
        dbg_SE = nc.dram_tensor("dbg_SE", [2, 128, KB], F32, kind="ExternalOutput").ap()
        dbg_el = nc.dram_tensor("dbg_el", [2, 128, 3 * 17], F32, kind="ExternalOutput").ap()

    A = nc.alloc_sbuf_tensor
    ident = A("ident", [128, 128], F32).ap()
    onesc = A("onesc", [1, 128], F32).ap()
    scols = A("scols", [128, NT], F32).ap()
    qcols = A("qcols", [128, NT], F32).ap()
    mcols = A("mcols", [128, NT], F32).ap()
    icols = A("icols", [128, NT], F32).ap()
    tmp1 = A("tmp1", [128, max(NT, 8)], F32).ap()
    tmp2 = A("tmp2", [128, max(NT, 8)], F32).ap()
    aff_tm = A("aff_tm", [128, E * JF], F32).ap()
    bqk_sb = A("bqk_sb", [128, 2 * DC], F32).ap()
    bv_sb = A("bv_sb", [1, D], F32).ap()
    bo_sb = A("bo_sb", [1, D], F32).ap()
    gw_sb = A("gw_sb", [128, DC * E], F32).ap()
    lo = A("lo", [128, E], F32).ap()
    hi = A("hi", [128, E], F32).ap()
    mid = A("mid", [128, E], F32).ap()
    cnt = A("cnt", [128, E], F32).ap()
    pred = A("pred", [128, E], F32).ap()
    dd1 = A("dd1", [128, E], F32).ap()
    msk = A("msk", [128, E * JF], F32).ap()
    iota1 = A("iota1", [128, E * JH], F32).ap()
    mif = A("mif", [128, E * JH], F32).ap()
    mi16 = A("mi16", [128, E * JH], I16).ap()
    wrap = A("wrap", [16, E * 8 * JH], I16).ap()
    idxs = A("idxs", [128, E * ICOLS], I16).ap()
    gidx = A("gidx", [128, E * ICOLS], I16).ap()
    sidx = A("sidx", [128, E * ICOLS], I16).ap()
    smsk = A("smsk", [128, E * ICOLS], I16).ap()
    nfound = A("nfound", [16, E], U32).ap()
    sio = A("sio", [16, ICOLS], F32).ap()
    Lcol = A("Lcol", [128, 3 * 16], F32).ap()
    ell = A("ell", [128, 3], F32).ap()
    sq_jnk = A("sq_jnk", [128, 1024], F32).ap()
    shf_sb = A("shf_sb", [128, 1], F32).ap()

    AR = 176 * 1024
    arena = A("arena", [128, AR], mybir.dt.uint8)
    abase = nc.lookup_mloc(arena).addr

    def at(name, shape, dtype, off):
        assert off + int(np.prod(shape[1:])) * mybir.dt.size(dtype) <= AR, name
        return nc.alloc_sbuf_tensor_at(name, shape, dtype, offset=abase + off).ap()

    fb = 4
    o = 0
    wq_sb = at("wq_sb", [128, DC * F], F32, o); o += DC * F * fb
    xc_sb = []
    for b in range(2):
        xc_sb.append(at(f"xc{b}", [128, 4 * D], F32, o)); o += 4 * D * fb
    xn_sb = at("xn_sb", [128, 4 * D], F32, o); o += 4 * D * fb
    xnT_sb = at("xnT_sb", [128, DC * 512], F32, o); o += DC * 512 * fb
    qke_sb = []
    for b in range(2):
        qke_sb.append(at(f"qke{b}", [128, 512], F32, o)); o += 512 * fb
    ve_sb = []
    for b in range(2):
        ve_sb.append(at(f"ve{b}", [128, D], F32, o)); o += D * fb
    afp_sb = at("afp_sb", [128, 4 * 64], F32, o); o += 4 * 64 * fb
    afc_sb = at("afc_sb", [128, 4 * E], F32, o); o += 4 * E * fb

    o = 0
    wo_sb = at("wo_sb", [128, DC * D], F32, o); o += DC * D * fb
    qt_sb = at("qt_sb", [128, DC * 384], F32, o); o += DC * 384 * fb
    kt_sb = []
    for b in range(2):
        kt_sb.append(at(f"kt{b}", [128, DC * KB], F32, o)); o += DC * KB * fb
    vb_sb = []
    for b in range(2):
        vb_sb.append(at(f"vb{b}", [128, NKC * D], F32, o)); o += NKC * D * fb
    E_sb = at("E_sb", [128, 3 * KB], F32, o); o += 3 * KB * fb
    et_sb = at("et_sb", [128, NKC * 384], F32, o); o += NKC * 384 * fb
    cx_sb = at("cx_sb", [128, 3 * D], F32, o); o += 3 * D * fb
    cxT_sb = at("cxT_sb", [128, DC * 384], F32, o); o += DC * 384 * fb
    xr_sb = at("xr_sb", [128, 3 * D], F32, o); o += 3 * D * fb
    ou_sb = at("ou_sb", [128, 3 * D], F32, o); o += 3 * D * fb

    o = 0
    gb_sb = []
    for b in range(2):
        gb_sb.append(at(f"gb{b}", [128, TPC * D], F32, o)); o += TPC * D * fb
    ab_sb = []
    for b in range(2):
        ab_sb.append(at(f"ab{b}", [128, TPC * 64], F32, o)); o += TPC * 64 * fb
    wg_sb = []
    for b in range(2):
        wg_sb.append(at(f"wg{b}", [128, DC * 128], F32, o)); o += DC * 128 * fb
    wu_sb = []
    for b in range(2):
        wu_sb.append(at(f"wu{b}", [128, DC * 128], F32, o)); o += DC * 128 * fb
    h_sb = at("h_sb", [128, IC * TCH], F32, o); o += IC * TCH * fb
    wd_sb = []
    for b in range(2):
        wd_sb.append(at(f"wd{b}", [128, IC * 128], F32, o)); o += IC * 128 * fb
    yt_sb = at("yt_sb", [128, DC * TCH], F32, o); o += DC * TCH * fb
    y_sb = at("y_sb", [128, TPC * D], F32, o); o += TPC * D * fb
    ga_sb = at("ga_sb", [128, TPC], F32, o); o += ((TPC * fb + 31) // 32) * 32
    BF16 = mybir.dt.bfloat16
    wgh_sb = []; wgl_sb = []; wuh_sb = []; wul_sb = []
    for b2 in range(2):
        wgh_sb.append(at(f"wgh{b2}", [128, DC * 128], BF16, o)); o += DC * 128 * 2
        wgl_sb.append(at(f"wgl{b2}", [128, DC * 128], BF16, o)); o += DC * 128 * 2
        wuh_sb.append(at(f"wuh{b2}", [128, DC * 128], BF16, o)); o += DC * 128 * 2
        wul_sb.append(at(f"wul{b2}", [128, DC * 128], BF16, o)); o += DC * 128 * 2
    xgh_sb = at("xgh_sb", [128, DC * TCH], BF16, o); o += DC * TCH * 2
    xgl_sb = at("xgl_sb", [128, DC * TCH], BF16, o); o += DC * TCH * 2
    WSW = min(512, DC * 128)
    wscr = at("wscr", [128, WSW], F32, o); o += WSW * fb

    p01 = nc.alloc_psum_tensor("p01", [128, 1024], F32).ap()
    p23 = nc.alloc_psum_tensor("p23", [128, 1024], F32).ap()
    p45 = nc.alloc_psum_tensor("p45", [128, 1024], F32).ap()
    p67 = nc.alloc_psum_tensor("p67", [128, 1024], F32).ap()

    P = Prog()
    V = P.val

    def mk(meth, *args, **kw):
        def fn(e):
            try:
                return getattr(e, meth)(*args, **kw)
            except Exception as ex:
                raise RuntimeError(f"emit failed: {meth} args={[getattr(a,'shape',a) for a in args]}") from ex
        return fn

    P.do("gpsimd", mk("load_library", libcfg.standard))
    P.do("gpsimd", mk("memset", ident[:, :], 1.0))
    P.do("gpsimd", mk("affine_select", ident[:, :], ident[:, :], [[-1, 128]],
                      Alu.is_equal, 0.0, base=0, channel_multiplier=1), inc="id_g")
    P.do("gpsimd", mk("iota", iota1[:, :], [[0, E], [128, JH]], base=1,
                      channel_multiplier=1, allow_small_or_imprecise_dtypes=True),
         inc="id_g")
    P.do("gpsimd", mk("iota", sio[:, :], [[16, ICOLS]], base=0,
                      channel_multiplier=1, allow_small_or_imprecise_dtypes=True),
         inc="id_g")
    P.do("vector", mk("memset", onesc[:, :], 1.0), inc="c_v")
    P.do("vector", mk("memset", gidx[:, :], 0), inc="c_v")
    P.do("vector", mk("memset", sidx[:, :], 0), inc="c_v")
    P.do("vector", mk("memset", Lcol[:, :], 0.0), inc="c_v")
    P.do("vector", mk("memset", shf_sb[:, :], -cfg.SHIFT), inc="c_v")
    P.do("vector", mk("memset", afp_sb[:, :], 0.0), inc="c_v")
    P.do("vector", mk("memset", idxs[:, :], 0), inc="c_v")
    P.do("vector", mk("memset", smsk[:, :], 0), inc="c_v")
    P.do("sync", mk("dma_start", out=bqk_sb[:, :],
                    in_=bqkv[0:2 * D].rearrange("(ft p) -> p ft", p=128),
                    allow_slow_non_contiguous=True), inc="c_ld", incn=16)
    P.do("sync", mk("dma_start", out=bv_sb[:, :], in_=bqkv[2 * D:3 * D][None, :]), inc="c_ld", incn=16)
    P.do("sync", mk("dma_start", out=bo_sb[:, :], in_=bout[:][None, :]), inc="c_ld", incn=16)
    P.do("sync", mk("dma_start", out=gw_sb[:, :].rearrange("p (dc e) -> p dc e", e=E),
                    in_=gwT[:, :].rearrange("(dc p) e -> p dc e", p=128)), inc="c_ld", incn=16)
    P.do("sync", mk("dma_start", out=wq_sb[:, :].rearrange("p (dc f) -> p dc f", dc=DC),
                    in_=wqkv[:, :].rearrange("(dc p) f -> p dc f", p=128)), inc="w_ld", incn=16)

    for j in range(NT):
        b = j % 2
        xt = xc_sb[b][:, 0:D]
        xls = f"x0l{j % 2}"
        P.do("sync", mk("dma_start", out=xt, in_=xf[128 * j:128 * j + 128, :]),
             wait=(("st_v", max(0, j - 1)), ("st_s", max(0, j - 1))), inc=xls, incn=16)
        P.do("vector", mk("reduce_sum", scols[:, j:j + 1], xt, axis=AX.X),
             wait=((xls, V(xls)),), inc="st_v")
        P.do("scalar", mk("activation", sq_jnk[:, 0:D], xt, Act.Square,
                          bias=0.0, scale=1.0, accum_out=qcols[:, j:j + 1]),
             wait=((xls, V(xls)),), inc="st_s")
    st_all = V("st_v")
    sts_all = V("st_s")
    P.do("vector", mk("tensor_scalar", mcols[:, :], scols[:, :], 1.0 / D, None, Alu.mult),
         wait=(("st_v", st_all), ("st_s", sts_all)))
    P.do("vector", mk("tensor_scalar", tmp1[:, 0:NT], qcols[:, :], 1.0 / D, None, Alu.mult))
    P.do("vector", mk("tensor_tensor", tmp2[:, 0:NT], mcols[:, :], mcols[:, :], Alu.mult))
    P.do("vector", mk("tensor_tensor", tmp1[:, 0:NT], tmp1[:, 0:NT], tmp2[:, 0:NT], Alu.subtract))
    P.do("vector", mk("tensor_scalar", tmp1[:, 0:NT], tmp1[:, 0:NT], cfg.EPS, None, Alu.add),
         inc="var_v")
    P.do("scalar", mk("activation", tmp2[:, 0:NT], tmp1[:, 0:NT], Act.Sqrt, bias=0.0, scale=1.0),
         wait=(("var_v", 1),), inc="sqr_sc")
    P.do("vector", mk("reciprocal", icols[:, :], tmp2[:, 0:NT]), wait=(("sqr_sc", 1),),
         inc="vself")
    P.wait("vector", "vself", V("vself"))
    for _ in range(2):
        P.do("vector", mk("tensor_tensor", tmp2[:, 0:NT], icols[:, :], icols[:, :], Alu.mult))
        P.do("vector", mk("tensor_tensor", tmp2[:, 0:NT], tmp2[:, 0:NT], tmp1[:, 0:NT], Alu.mult))
        P.do("vector", mk("tensor_scalar", tmp2[:, 0:NT], tmp2[:, 0:NT], -0.5, 1.5, Alu.mult, Alu.add))
        P.do("vector", mk("tensor_tensor", icols[:, :], icols[:, :], tmp2[:, 0:NT], Alu.mult))
    P.do("vector", mk("engine_nop"), inc="is_v")
    if debug:
        for di, src_t in enumerate((scols, qcols, mcols, icols, tmp1)):
            P.do("scalar", mk("dma_start", out=dbg_st1[:, di * NT:(di + 1) * NT],
                              in_=src_t[:, 0:NT]),
                 wait=(("is_v", 1),), inc="dbg_st", incn=16)
        P.wait("scalar", "dbg_st", V("dbg_st"))

    if pend >= 1:
        xnst_hist = [0] * (NCH + 1)
        qst_hist = []
        vst_hist = []
        for ci in range(NCH):
            cb = ci % 2
            own = (512 * ci) < SO
            for jj in range(4):
                j = 4 * ci + jj
                P.do("sync", mk("dma_start", out=xc_sb[cb][:, jj * D:(jj + 1) * D],
                                in_=xf[128 * j:128 * j + 128, :]),
                     wait=(("st_v", st_all if ci <= 1 else 0),
                           ("st_s", sts_all if ci <= 1 else 0),
                           ("xn_v", max(0, 4 * (ci - 1))),), inc=f"xl{ci % 2}", incn=16)
            tp_before = V("tp_mm")
            for jj in range(4):
                j = 4 * ci + jj
                P.do("vector", mk("tensor_scalar", xn_sb[:, jj * D:(jj + 1) * D],
                                  xc_sb[cb][:, jj * D:(jj + 1) * D],
                                  mcols[:, j:j + 1], icols[:, j:j + 1],
                                  Alu.subtract, Alu.mult),
                     wait=((f"xl{ci % 2}", V(f"xl{ci % 2}")), ("is_v", 1),
                           ("tp_mm", tp_before), ("xnst", xnst_hist[ci - 1] if ci > 0 else 0)),
                     inc="xn_v", ss=True)
                P.do("scalar", mk("dma_start", out=xn_h[128 * j:128 * j + 128, :],
                                  in_=xn_sb[:, jj * D:(jj + 1) * D]),
                     wait=(("xn_v", V("xn_v")),), inc="xnst", incn=16)
            xnst_hist[ci] = V("xnst")
            lg_ev_prev = V("lg_ev")
            lg_mm_prev = V("lg_mm")
            for dc in range(DC):
                hb = dc % 2
                bank = p01[:, hb * 512: hb * 512 + 512]
                for jj in range(4):
                    P.do("tensor", mk("transpose", bank[:, jj * 128:(jj + 1) * 128],
                                      xn_sb[:, jj * D + dc * 128: jj * D + dc * 128 + 128],
                                      ident[:, :]),
                         wait=(("xn_v", 4 * ci + 4), ("id_g", 1),
                               ("te_v", max(0, V("te_v") - 1)),
                               ("ex_sc", V("ex_sc") if dc <= 1 else 0),
                               ),
                         inc="tp_mm" if jj == 3 else None)
                P.do("vector", mk("tensor_copy", xnT_sb[:, dc * 512:(dc + 1) * 512], bank),
                     wait=(("tp_mm", V("tp_mm")), ("lg_mm", lg_mm_prev)),
                     inc="te_v", ss=True)
            te_done = V("te_v")
            fts = (list(range(2 * DC)) if own else list(range(DC, 2 * DC)))
            for gi, ft in enumerate(fts):
                hb = gi % 2
                bank = p23[:, hb * 512: hb * 512 + 512]
                for dc in range(DC):
                    P.do("tensor", mk("matmul", bank,
                                      wq_sb[:, dc * F + ft * 128: dc * F + ft * 128 + 128],
                                      xnT_sb[:, dc * 512:(dc + 1) * 512],
                                      start=(dc == 0), stop=(dc == DC - 1)),
                         wait=(("te_v", te_done), ("w_ld", 16), ("c_ld", 64),
                               ("qe_v", max(0, V("qe_v") - 1))),
                         inc="qk_mm" if dc == DC - 1 else None)
                g = len(qst_hist)
                eb = qke_sb[gi % 2]
                P.do("vector", mk("tensor_scalar", eb[:, :], bank, bqk_sb[:, ft:ft + 1],
                                  None, Alu.add),
                     wait=(("qk_mm", V("qk_mm")),
                           ((f"qs{g % 2}"), V(f"qs{g % 2}"))),
                     inc="qe_v", ss=True)
                if ft < DC:
                    dst = qT_h[ft, :, 512 * ci: 512 * ci + 512]
                else:
                    dst = kT_h[ft - DC, :, 512 * ci: 512 * ci + 512]
                P.do("scalar", mk("dma_start", out=dst, in_=eb[:, :]),
                     wait=(("qe_v", V("qe_v")),), inc=f"qs{g % 2}", incn=16)
                qst_hist.append(0)
            for jj in range(4):
                j = 4 * ci + jj
                vt = 4 * ci + jj
                vp = [p45, p67][vt % 2]
                for dc in range(DC):
                    for hv in range(NHV):
                        P.do("tensor", mk("matmul", vp[:, hv * 512:hv * 512 + NW],
                                          xnT_sb[:, dc * 512 + jj * 128: dc * 512 + jj * 128 + 128],
                                          wq_sb[:, dc * F + 2 * D + hv * 512: dc * F + 2 * D + hv * 512 + NW],
                                          start=(dc == 0), stop=False),
                             wait=(("ve_v", max(0, V("ve_v") - 1)),))
                for hv in range(NHV):
                    P.do("tensor", mk("matmul", vp[:, hv * 512:hv * 512 + NW],
                                      onesc[:, :], bv_sb[:, hv * 512:hv * 512 + NW],
                                      start=False, stop=True),
                         inc="v_mm" if hv == NHV - 1 else None)
                g = len(vst_hist)
                vbuf = ve_sb[jj % 2]
                P.do("vector", mk("tensor_copy", vbuf[:, 0:D], vp[:, 0:D]),
                     wait=(("v_mm", V("v_mm")),
                           ((f"vs{jj % 2}"), V(f"vs{jj % 2}"))),
                     inc="ve_v", ss=True)
                P.do("scalar", mk("dma_start", out=v_h[128 * j:128 * j + 128, :], in_=vbuf[:, 0:D]),
                     wait=(("ve_v", V("ve_v")),), inc=f"vs{jj % 2}", incn=16)
                vst_hist.append(0)
                lg = p01[:, jj * E: jj * E + E]
                for dc in range(DC):
                    P.do("tensor", mk("matmul", lg,
                                      xnT_sb[:, dc * 512 + jj * 128: dc * 512 + jj * 128 + 128],
                                      gw_sb[:, dc * E:(dc + 1) * E],
                                      start=(dc == 0), stop=(dc == DC - 1)),
                         wait=(("te_v", te_done), ("ex_sc", V("ex_sc")),),
                         inc="lg_mm" if dc == DC - 1 else None)
                P.do("vector", mk("reduce_max", tmp1[:, 0:1], lg, axis=AX.X),
                     wait=(("lg_mm", V("lg_mm")),), ss=True)
                P.do("vector", mk("tensor_scalar", tmp1[:, 0:1], tmp1[:, 0:1], -1.0, None, Alu.mult),
                     inc="mx_v", ss=True)
                P.do("scalar", mk("activation", afc_sb[:, jj * E:(jj + 1) * E], lg, Act.Exp,
                                  bias=tmp1[:, 0:1], scale=1.0, accum_out=tmp2[:, 0:1]),
                     wait=(("mx_v", V("mx_v")),), inc="ex_sc")
                P.do("vector", mk("reciprocal", tmp2[:, 1:2], tmp2[:, 0:1]),
                     wait=(("ex_sc", V("ex_sc")),), inc="vself")
                P.wait("vector", "vself", V("vself"))
                P.do("vector", mk("tensor_scalar", afc_sb[:, jj * E:(jj + 1) * E],
                                  afc_sb[:, jj * E:(jj + 1) * E], tmp2[:, 1:2], None, Alu.mult), ss=True)
                P.do("vector", mk("tensor_copy",
                                  aff_tm[:, j: j + (E - 1) * JF + 1: JF],
                                  afc_sb[:, jj * E:(jj + 1) * E]))
                P.do("vector", mk("tensor_copy", afp_sb[:, jj * 64: jj * 64 + E],
                                  afc_sb[:, jj * E:(jj + 1) * E]),
                     wait=(("afst", V("afst") if False else 0),), inc="lg_ev")
            P.do("scalar", mk("dma_start",
                              out=af_h[512 * ci:512 * ci + 512, :].rearrange("(a p) f -> p a f", p=128),
                              in_=afp_sb[:, :].rearrange("p (a f) -> p a f", f=64)),
                 wait=(("lg_ev", V("lg_ev")),), inc="afst", incn=16)

    p1_xnst = V("xnst")
    p1_qs0, p1_qs1 = V("qs0"), V("qs1")
    p1_vs0, p1_vs1 = V("vs0"), V("vs1")
    p1_afst = V("afst")
    p1_lgmm = V("lg_mm")
    p1_vmm = V("v_mm")
    p1_exsc = V("ex_sc")
    aff_done = V("lg_ev")
    if pend == 0:
        P.wait("vector", "st_v", V("st_v"))
        P.wait("scalar", "st_s", V("st_s"))
    if pend == 1:
        P.wait("scalar", "xnst", V("xnst"))
        P.wait("scalar", "qs0", V("qs0"))
        P.wait("scalar", "qs1", V("qs1"))
        P.wait("scalar", "vs0", V("vs0"))
        P.wait("scalar", "vs1", V("vs1"))
        P.wait("scalar", "afst", V("afst"))

    if pend >= 2:
        aff3 = aff_tm.rearrange("p (e j) -> p e j", e=E)
        msk3 = msk.rearrange("p (e j) -> p e j", e=E)
        P.do("gpsimd", mk("load_library", libcfg.attn))
        P.do("vector", mk("memset", lo[:, :], 0.0), wait=(("lg_ev", aff_done),))
        P.do("vector", mk("memset", hi[:, :], 1.0))
        for it in range(cfg.NITER):
            if it > 0:
                P.do("vector", mk("tensor_scalar", pred[:, :], cnt[:, :], float(cfg.KSEL), None, Alu.is_ge),
                     wait=(("rt_gv", it),), ss=True)
                P.do("vector", mk("tensor_tensor", dd1[:, :], mid[:, :], lo[:, :], Alu.subtract), ss=True)
                P.do("vector", mk("tensor_tensor", dd1[:, :], pred[:, :], dd1[:, :], Alu.mult), ss=True)
                P.do("vector", mk("tensor_tensor", lo[:, :], lo[:, :], dd1[:, :], Alu.add), ss=True)
                P.do("vector", mk("tensor_tensor", dd1[:, :], hi[:, :], mid[:, :], Alu.subtract), ss=True)
                P.do("vector", mk("tensor_tensor", dd1[:, :], pred[:, :], dd1[:, :], Alu.mult), ss=True)
                P.do("vector", mk("tensor_tensor", hi[:, :], mid[:, :], dd1[:, :], Alu.add), ss=True)
            P.do("vector", mk("tensor_tensor", mid[:, :], lo[:, :], hi[:, :], Alu.add), ss=True)
            P.do("vector", mk("tensor_scalar", mid[:, :], mid[:, :], 0.5, None, Alu.mult), ss=True)
            P.do("vector", mk("tensor_tensor", msk3[:, :, :], aff3[:, :, :],
                              mid.unsqueeze(2).broadcast_to([128, E, JF]), Alu.is_ge), ss=True)
            P.do("vector", mk("tensor_reduce", cnt[:, :], msk3[:, :, :], op=Alu.add, axis=AX.X),
                 inc="rt_vg")
            P.do("gpsimd", mk("partition_all_reduce", cnt[:, :], cnt[:, :], 128, ReduceOp.add),
                 wait=(("rt_vg", it + 1), ("id_g", 2)), inc="rt_gv")
        P.do("vector", mk("tensor_scalar", pred[:, :], cnt[:, :], float(cfg.KSEL), None, Alu.is_ge),
             wait=(("rt_gv", cfg.NITER),), ss=True)
        P.do("vector", mk("tensor_tensor", dd1[:, :], mid[:, :], lo[:, :], Alu.subtract), ss=True)
        P.do("vector", mk("tensor_tensor", dd1[:, :], pred[:, :], dd1[:, :], Alu.mult), ss=True)
        P.do("vector", mk("tensor_tensor", lo[:, :], lo[:, :], dd1[:, :], Alu.add), inc="lo_v", ss=True)
        affo3 = aff_tm.rearrange("p (e j) -> p e j", e=E)[:, :, 0:JH]
        mif3 = mif.rearrange("p (e j) -> p e j", e=E)
        P.do("vector", mk("tensor_tensor", mif3[:, :, :], affo3[:, :, :],
                          lo.unsqueeze(2).broadcast_to([128, E, JH]), Alu.is_ge), ss=True)
        P.do("vector", mk("tensor_tensor", mif[:, :], mif[:, :], iota1[:, :], Alu.mult),
             wait=(("id_g", 2),), ss=True)
        P.do("vector", mk("tensor_scalar", mif[:, :], mif[:, :], -1.0, None, Alu.add), ss=True)
        P.do("vector", mk("tensor_copy", mi16[:, :], mif[:, :]), inc="msk_v", ss=True)
        for g in range(8):
            src = mi16[16 * g:16 * g + 16, :].rearrange("p (e j) -> p e j", e=E)
            dst = wrap[0:16, :].rearrange("p (e g j) -> p e g j", e=E, g=8)[:, :, g, :]
            P.do("sync", mk("dma_start", out=dst, in_=src),
                 wait=(("msk_v", V("msk_v")),), inc="fold_st", incn=16)
        P.do("gpsimd", mk("load_library", libcfg.sparse_gather))
        for e in range(E):
            P.do("gpsimd", mk("sparse_gather", idxs[0:16, e * ICOLS:(e + 1) * ICOLS],
                              wrap[:, e * 8 * JH:(e + 1) * 8 * JH],
                              num_found=nfound[0:1, e:e + 1]),
                 wait=(("fold_st", V("fold_st")),), inc="sg_g")
        P.do("sync", mk("dma_start", out=nfound[1:2, :], in_=nfound[0:1, :]),
             wait=(("sg_g", V("sg_g")),), inc="nfb_st", incn=16)
        P.do("sync", mk("dma_start", out=nfound[2:4, :], in_=nfound[0:2, :]),
             wait=(("nfb_st", 16),), inc="nfb_st", incn=16)
        P.do("sync", mk("dma_start", out=nfound[4:8, :], in_=nfound[0:4, :]),
             wait=(("nfb_st", 32),), inc="nfb_st", incn=16)
        P.do("sync", mk("dma_start", out=nfound[8:16, :], in_=nfound[0:8, :]),
             wait=(("nfb_st", 48),), inc="nfb_st", incn=16)
        nf_f = cnt
        P.do("vector", mk("tensor_copy", nf_f[0:16, 0:E], nfound[:, :]),
             wait=(("nfb_st", 64), ("sg_g", V("sg_g")),), ss=True)
        vmq = mif[0:16, 0:ICOLS]
        ivf = mif[0:16, ICOLS:2 * ICOLS]
        for e in range(E):
            P.do("vector", mk("tensor_scalar", vmq, sio[:, :], nf_f[0:16, e:e + 1],
                              None, Alu.is_lt),
                 wait=(("id_g", 3),), ss=True)
            P.do("vector", mk("tensor_copy", ivf, idxs[0:16, e * ICOLS:(e + 1) * ICOLS]),
                 ss=True)
            P.do("vector", mk("tensor_scalar", ivf, ivf, 1.0, None, Alu.add), ss=True)
            P.do("vector", mk("tensor_tensor", ivf, ivf, vmq, Alu.mult), ss=True)
            P.do("vector", mk("tensor_scalar", ivf, ivf, -1.0, None, Alu.add), ss=True)
            P.do("vector", mk("tensor_copy", idxs[0:16, e * ICOLS:(e + 1) * ICOLS], ivf),
                 inc="cln_v", ss=True)
        P.do("vector", mk("tensor_scalar", gidx[0:16, :], idxs[0:16, :], 0, 0, Alu.max, Alu.add),
             wait=(("cln_v", V("cln_v")),), ss=True)
        P.do("vector", mk("tensor_scalar", smsk[0:16, :], idxs[0:16, :], 0, SO + 1, Alu.is_lt, Alu.mult), ss=True)
        P.do("vector", mk("tensor_tensor", sidx[0:16, :], idxs[0:16, :], smsk[0:16, :], Alu.add),
             inc="idx_v", ss=True)
        P.do("gpsimd", mk("dma_start", out=gidx[16:32, :], in_=gidx[0:16, :]),
             wait=(("idx_v", 1),), inc="rep_st", incn=16)
        P.do("gpsimd", mk("dma_start", out=sidx[16:32, :], in_=sidx[0:16, :]),
             inc="rep_st", incn=16)
        P.do("gpsimd", mk("load_library", libcfg.mlp))
        if debug:
            P.do("gpsimd", mk("dma_start", out=dbg_bi[:, 0:E], in_=cnt[:, :]), inc="dbg_g", incn=16)
            P.do("gpsimd", mk("dma_start", out=dbg_bi[:, E:2 * E], in_=hi[:, :]), inc="dbg_g", incn=16)
            P.do("gpsimd", mk("dma_start", out=dbg_bi[:, 2 * E:3 * E], in_=msk[:, 0:E]), inc="dbg_g", incn=16)
            P.do("gpsimd", mk("dma_start", out=dbg_aff[:, :], in_=aff_tm[:, :]), inc="dbg_g", incn=16)
            P.do("gpsimd", mk("dma_start", out=dbg_lo[:, :], in_=lo[:, :]),
                 wait=(("lo_v", 1),), inc="dbg_g", incn=16)
            P.do("gpsimd", mk("dma_start", out=dbg_nf[:, :], in_=nfound[0:1, :]), inc="dbg_g", incn=16)
            P.do("gpsimd", mk("dma_start", out=dbg_idx[:, :], in_=idxs[:, :]), inc="dbg_g", incn=16)


    if pend >= 3:
        P.do("sync", mk("dma_start", out=wo_sb[:, :].rearrange("p (dc f) -> p dc f", dc=DC),
                        in_=wout[:, :].rearrange("(dc p) f -> p dc f", p=128)),
             wait=(("lg_mm", p1_lgmm), ("v_mm", p1_vmm), ("ex_sc", p1_exsc),
                   ("lo_v", 1)),
             inc="w3_ld", incn=16)
        ctx_banks = [p01, p23, p45]
        kt_hist = []
        vbl_hist = []
        ou_slot_hist = {}
        q0 = 0
        for qi, nqt in enumerate(cfg.QCS):
            WQ = 128 * nqt
            P.do("sync", mk("dma_start", out=qt_sb[:, 0:DC * WQ].rearrange("p (dc c) -> p dc c", dc=DC),
                            in_=qT_h[:, :, q0:q0 + WQ].rearrange("dc p c -> p dc c")),
                 wait=(("qs0", p1_qs0), ("qs1", p1_qs1), ("xnst", p1_xnst),
                       ("vs0", p1_vs0), ("vs1", p1_vs1),
                       ("op_mm", V("op_mm"))),
                 inc="q_ld", incn=16)
            q_ld_here = V("q_ld")
            for kb in range(NKB):
                gkb = len(kt_hist)
                pb = gkb % 2
                P.do("sync", mk("dma_start",
                                out=kt_sb[pb][:, :].rearrange("p (dc c) -> p dc c", dc=DC),
                                in_=kT_h[:, :, KB * kb:KB * kb + KB].rearrange("dc p c -> p dc c")),
                     wait=(("ete_v", kt_hist[gkb - 2] if gkb >= 2 else 0),),
                     inc=f"ktl{pb}", incn=16)
                P.do("sync", mk("dma_start",
                                out=vb_sb[pb][:, :].rearrange("p (a f) -> p a f", a=NKC),
                                in_=v_h[KB * kb:KB * kb + KB, :].rearrange("(a p) f -> p a f", p=128)),
                     wait=(("cx_mm", vbl_hist[gkb - 2] if gkb >= 2 else 0),),
                     inc=f"vbl{pb}", incn=16)
                kt_here = V(f"ktl{pb}")
                vb_here = V(f"vbl{pb}")
                cte_prev = V("cte_v")
                for qt in range(nqt):
                    for dc in range(DC):
                        P.do("tensor", mk("matmul", p67[:, 0:KB],
                                          qt_sb[:, dc * WQ + qt * 128: dc * WQ + qt * 128 + 128],
                                          kt_sb[pb][:, dc * KB:(dc + 1) * KB],
                                          start=(dc == 0), stop=(dc == DC - 1)),
                             wait=((f"ktl{pb}", kt_here), ("q_ld", q_ld_here),
                                   ("ex3_sc", V("ex3_sc")), ("cte_v", cte_prev)),
                             inc="s_mm" if dc == DC - 1 else None)
                    P.do("scalar", mk("activation", E_sb[:, qt * KB:(qt + 1) * KB],
                                      p67[:, 0:KB], Act.Exp, bias=shf_sb[:, 0:1],
                                      scale=1.0 / math.sqrt(D),
                                      accum_out=Lcol[:, qt * 16 + kb: qt * 16 + kb + 1]),
                         wait=(("s_mm", V("s_mm")), ("etm_mm", V("etm_mm")), ("c_v", 5)),
                         inc="ex3_sc", ss=True)
                    if debug and qi == 0 and kb == 0 and qt == 0:
                        P.do("scalar", mk("dma_start", out=dbg_SE[1, :, :],
                                          in_=E_sb[:, 0:KB]),
                             wait=(("ex3_sc", V("ex3_sc")),), inc="dbg_st", incn=16)
                ex_here = V("ex3_sc")
                for kc in range(NKC):
                    for qt in range(nqt):
                        P.do("tensor", mk("transpose",
                                          p67[:, 512 + qt * 128: 512 + qt * 128 + 128],
                                          E_sb[:, qt * KB + kc * 128: qt * KB + kc * 128 + 128],
                                          ident[:, :]),
                             wait=(("ex3_sc", ex_here), ("ete_v", V("ete_v")),),
                             inc="etm_mm" if qt == nqt - 1 else None)
                    P.do("vector", mk("tensor_copy", et_sb[:, kc * 384: kc * 384 + WQ],
                                      p67[:, 512:512 + WQ]),
                         wait=(("etm_mm", V("etm_mm")), ("cx_mm", V("cx_mm"))),
                         inc="ete_v", ss=True)
                    ete_here = V("ete_v")
                    for qt in range(nqt):
                        for hv in range(NHV):
                            P.do("tensor", mk("matmul",
                                              ctx_banks[qt][:, hv * 512: hv * 512 + NW],
                                              et_sb[:, kc * 384 + qt * 128: kc * 384 + qt * 128 + 128],
                                              vb_sb[pb][:, kc * D + hv * 512: kc * D + hv * 512 + NW],
                                              start=(kb == 0 and kc == 0),
                                              stop=(kb == NKB - 1 and kc == NKC - 1)),
                                 wait=(("ete_v", ete_here), (f"vbl{pb}", vb_here),
                                   ("oev_v", V("oev_v"))),
                                 inc="cx_mm" if (qt == nqt - 1 and hv == NHV - 1) else None)
                kt_hist.append(V("ete_v"))
                vbl_hist.append(V("cx_mm"))
            cx_done = V("cx_mm")
            for qt in range(nqt):
                P.do("vector", mk("reduce_sum", ell[:, qt:qt + 1],
                                  Lcol[:, qt * 16: qt * 16 + NKB], axis=AX.X),
                     wait=(("cx_mm", cx_done),), ss=True)
                P.do("vector", mk("reciprocal", ell[:, qt:qt + 1], ell[:, qt:qt + 1]),
                     inc="vself")
                P.wait("vector", "vself", V("vself"))
                P.do("vector", mk("tensor_scalar", cx_sb[:, qt * D:(qt + 1) * D],
                                  ctx_banks[qt][:, 0:D], ell[:, qt:qt + 1], None, Alu.mult),
                     inc="cb_v", ss=True)
            cb_here = V("cb_v")
            if debug and qi < 2:
                P.do("scalar", mk("dma_start", out=dbg_cx[qi, :, :], in_=cx_sb[:, 0:3 * D]),
                     wait=(("cb_v", V("cb_v")),), inc="dbg_st", incn=16)
                P.do("scalar", mk("dma_start", out=dbg_el[qi, :, 0:48],
                                  in_=Lcol[:, 0:48]), inc="dbg_st", incn=16)
                P.do("scalar", mk("dma_start", out=dbg_el[qi, :, 48:51],
                                  in_=ell[:, 0:3]), inc="dbg_st", incn=16)
            for dc in range(DC):
                hb = dc % 2
                bank = p67[:, hb * 512: hb * 512 + 512]
                for qt in range(nqt):
                    P.do("tensor", mk("transpose", bank[:, qt * 128:(qt + 1) * 128],
                                      cx_sb[:, qt * D + dc * 128: qt * D + dc * 128 + 128],
                                      ident[:, :]),
                         wait=(("cb_v", cb_here), ("cte_v", max(0, V("cte_v") - 1)),),
                         inc="ctm_mm" if qt == nqt - 1 else None)
                P.do("vector", mk("tensor_copy", cxT_sb[:, dc * 384: dc * 384 + WQ],
                                  bank[:, 0:WQ]),
                     wait=(("ctm_mm", V("ctm_mm")),), inc="cte_v", ss=True)
            cte_here = V("cte_v")
            P.do("sync", mk("dma_start",
                            out=xr_sb[:, 0:nqt * D].rearrange("p (a f) -> p a f", a=nqt),
                            in_=xf[q0:q0 + WQ, :].rearrange("(a p) f -> p a f", p=128)),
                 wait=(("oev_v", V("oev_v")),),
                 inc="res_ld", incn=16)
            res_here = V("res_ld")
            for qt in range(nqt):
                bank = ctx_banks[qt]
                for hv in range(NHV):
                    for dc in range(DC):
                        P.do("tensor", mk("matmul", bank[:, hv * 512: hv * 512 + NW],
                                          cxT_sb[:, dc * 384 + qt * 128: dc * 384 + qt * 128 + 128],
                                          wo_sb[:, dc * D + hv * 512: dc * D + hv * 512 + NW],
                                          start=(dc == 0), stop=False),
                             wait=(("cte_v", cte_here), ("w3_ld", 16),))
                    P.do("tensor", mk("matmul", bank[:, hv * 512: hv * 512 + NW],
                                      onesc[:, :], bo_sb[:, hv * 512: hv * 512 + NW],
                                      start=False, stop=True),
                         inc="op_mm" if hv == NHV - 1 else None)
                P.do("vector", mk("tensor_tensor", ou_sb[:, qt * D:(qt + 1) * D],
                                  bank[:, 0:D], xr_sb[:, qt * D:(qt + 1) * D], Alu.add),
                     wait=(("op_mm", V("op_mm")), ("res_ld", res_here),
                           ("out_st", V("out_st"))),
                     inc="oev_v", ss=True)
                P.do("scalar", mk("dma_start", out=out[q0 + qt * 128: q0 + qt * 128 + 128, :],
                                  in_=ou_sb[:, qt * D:(qt + 1) * D]),
                     wait=(("oev_v", V("oev_v")),), inc="out_st", incn=16)
                ou_slot_hist[qt] = V("out_st")
            q0 += WQ
        out_rows_done = V("out_st")
        p3_opmm = V("op_mm")
        p3_oev = V("oev_v")
        p3_cte = V("cte_v")


    if pend >= 4:
        RREG = {}

        def _mkreg(e):
            RREG["t"] = e.to_reg(TCH)
            return None
        P.do("gpsimd", _mkreg)

        def mk_gather(dst, srch, icol0):
            def fn(e):
                return e.dma_gather(dst, srch, gidx[:, icol0:icol0 + ICC], TCH, RREG["t"],
                                    dst.shape[-1])
            return fn

        def mk_scatter(icol0):
            def fn(e):
                return e.dma_scatter_add(out[:, :], y_sb[:, :].rearrange("p (a f) -> p a f", f=D),
                                         sidx[:, icol0:icol0 + ICC], TCH, RREG["t"], D)
            return fn
        gt_hist = []
        ga_hist = []
        gu_hist = []
        dn_hist = []
        for e in range(E):
            for c in range(NTCH):
                ch = e * NTCH + c
                gpar = ch % 2
                icol0 = e * ICOLS + c * ICC
                gu_hist.append([])
                dn_hist.append([])
                P.do("gpsimd", mk_gather(gb_sb[gpar][:, :].rearrange("p (a f) -> p a f", f=D),
                                         xn_h[:, :], icol0),
                     wait=(("rep_st", 32), ("xnst", p1_xnst), ("afst", p1_afst),
                           ("op_mm", p3_opmm), ("oev_v", p3_oev),
                           ("gt_mm", gt_hist[ch - 2] if ch >= 2 else 0),
                           ("ga_v", ga_hist[ch - 2] if ch >= 2 else 0)),
                     inc=f"gd{gpar}", incn=16)
                P.do("gpsimd", mk_gather(ab_sb[gpar][:, :].rearrange("p (a f) -> p a f", f=64),
                                         af_h[:, :], icol0),
                     inc=f"gd{gpar}", incn=16)
                gds = f"gd{gpar}"
                g_here = V(gds)
                P.do("vector", mk("tensor_copy", ga_sb[:, 0:TPC],
                                  ab_sb[gpar][:, e: e + (TPC - 1) * 64 + 1: 64] if TPC > 1
                                  else ab_sb[gpar][:, e:e + 1]),
                     wait=((gds, g_here), ("ysc_v", V("ysc_v")),), inc="ga_v")
                ga_hist.append(V("ga_v"))
                gu_prev_chunk = V("gu_mm")
                for dc in range(DC):
                    hb = dc % 2
                    bank = p45[:, hb * 512: hb * 512 + 512]
                    for tt in range(TPC):
                        P.do("tensor", mk("transpose", bank[:, tt * 128:(tt + 1) * 128],
                                          gb_sb[gpar][:, tt * D + dc * 128: tt * D + dc * 128 + 128],
                                          ident[:, :]),
                             wait=((gds, g_here),
                                   ("gte_v", max(0, V("gte_v") - 1)),
                                   ("ysc_v", V("ysc_v")),),
                             inc="gt_mm" if tt == TPC - 1 else None)
                    xs_ = slice(dc * TCH, (dc + 1) * TCH)
                    P.do("vector", mk("tensor_copy", xgh_sb[:, xs_], bank[:, 0:TCH]),
                         wait=(("gt_mm", V("gt_mm")), ("gu_mm", gu_prev_chunk)),
                         ss=True)
                    P.do("vector", mk("tensor_copy", wscr[:, 0:min(WSW, TCH)],
                                      xgh_sb[:, xs_]), ss=True)
                    P.do("vector", mk("tensor_tensor", wscr[:, 0:min(WSW, TCH)],
                                      bank[:, 0:TCH], wscr[:, 0:min(WSW, TCH)],
                                      Alu.subtract), ss=True)
                    P.do("vector", mk("tensor_copy", xgl_sb[:, xs_],
                                      wscr[:, 0:min(WSW, TCH)]),
                         inc="gte_v", ss=True)
                gt_hist.append(V("gt_mm"))
                gte_here = V("gte_v")
                dn_prev_chunk = V("dn_mm")
                for it in range(IC):
                    wpar = it % 2
                    P.do("sync", mk("dma_start",
                                    out=wg_sb[wpar][:, :].rearrange("p (dc i) -> p dc i", dc=DC),
                                    in_=wgT[e, it, :, :, :]),
                         wait=(("gu_mm", gu_hist[ch][it - 2] if it >= 2 else
                                (gu_hist[ch - 1][IC - 2 + it] if ch >= 1 else 0)),
                               ("oev_v", p3_oev), ("cte_v", p3_cte)),
                         inc=f"wt{wpar}", incn=16)
                    P.do("sync", mk("dma_start",
                                    out=wu_sb[wpar][:, :].rearrange("p (dc i) -> p dc i", dc=DC),
                                    in_=wuT[e, it, :, :, :]),
                         inc=f"wt{wpar}", incn=16)
                    wt_here = V(f"wt{wpar}")
                    NWS = (DC * 128) // WSW
                    for srcb, hb_, lb_ in ((wg_sb[wpar], wgh_sb[wpar], wgl_sb[wpar]),
                                           (wu_sb[wpar], wuh_sb[wpar], wul_sb[wpar])):
                        P.do("scalar", mk("activation", hb_[:, :], srcb[:, :], Act.Copy,
                                          bias=0.0, scale=1.0),
                             wait=((f"wt{wpar}", wt_here), ("wlo_sc", V("wlo_sc"))),
                             inc="whi_sc", ss=True)
                        for hseg in range(NWS):
                            sg = slice(hseg * WSW, (hseg + 1) * WSW)
                            P.do("vector", mk("tensor_tensor", wscr[:, :], srcb[:, sg],
                                              hb_[:, sg], Alu.subtract),
                                 wait=(("whi_sc", V("whi_sc")), ("wlo_sc", V("wlo_sc"))),
                                 inc="wsub_v", ss=True)
                            P.do("scalar", mk("activation", lb_[:, sg], wscr[:, :], Act.Copy,
                                              bias=0.0, scale=1.0),
                                 wait=(("wsub_v", V("wsub_v")),), inc="wlo_sc", ss=True)
                    wsp_here = V("wlo_sc")
                    gu_bank = [p01, p23][it % 2]
                    for p3p, (wl_, xl_) in enumerate(((wgh_sb[wpar], xgh_sb),
                                                      (wgl_sb[wpar], xgh_sb),
                                                      (wgh_sb[wpar], xgl_sb))):
                        for dc in range(DC):
                            P.do("tensor", mk("matmul", gu_bank[:, 0:TCH],
                                              wl_[:, dc * 128:(dc + 1) * 128],
                                              xl_[:, dc * TCH:(dc + 1) * TCH],
                                              start=(p3p == 0 and dc == 0),
                                              stop=(p3p == 2 and dc == DC - 1)),
                                 wait=(("gte_v", gte_here), ("wlo_sc", wsp_here),
                                       ("h_v", max(0, V("h_v") - 1)),),
                                 inc="gu_mm" if (p3p == 2 and dc == DC - 1) else None)
                    for p3p, (wl_, xl_) in enumerate(((wuh_sb[wpar], xgh_sb),
                                                      (wul_sb[wpar], xgh_sb),
                                                      (wuh_sb[wpar], xgl_sb))):
                        for dc in range(DC):
                            P.do("tensor", mk("matmul", gu_bank[:, 512:512 + TCH],
                                              wl_[:, dc * 128:(dc + 1) * 128],
                                              xl_[:, dc * TCH:(dc + 1) * TCH],
                                              start=(p3p == 0 and dc == 0),
                                              stop=(p3p == 2 and dc == DC - 1)),
                                 inc="gu_mm" if (p3p == 2 and dc == DC - 1) else None)
                    gu_hist[ch].append(V("gu_mm"))
                    P.do("scalar", mk("activation", h_sb[:, it * TCH:(it + 1) * TCH],
                                      gu_bank[:, 0:TCH], Act.Silu, bias=0.0, scale=1.0),
                         wait=(("gu_mm", V("gu_mm")), ("dn_mm", dn_prev_chunk)),
                         inc="h_sc", ss=True)
                    P.do("vector", mk("tensor_tensor", h_sb[:, it * TCH:(it + 1) * TCH],
                                      h_sb[:, it * TCH:(it + 1) * TCH],
                                      gu_bank[:, 512:512 + TCH], Alu.mult),
                         wait=(("h_sc", V("h_sc")),), inc="h_v", ss=True)
                h_here = V("h_v")
                yt_prev_chunk = V("yt_mm")
                for dt in range(DC):
                    wpar = dt % 2
                    P.do("sync", mk("dma_start",
                                    out=wd_sb[wpar][:, :].rearrange("p (ic d) -> p ic d", ic=IC),
                                    in_=wdT[e, dt, :, :, :]),
                         wait=(("dn_mm", dn_hist[ch][dt - 2] if dt >= 2 else
                                (dn_hist[ch - 1][DC - 2 + dt] if ch >= 1 else 0)),),
                         inc=f"wdl{wpar}", incn=16)
                    wd_here = V(f"wdl{wpar}")
                    dn_bank = p67[:, (dt % 2) * 512: (dt % 2) * 512 + TCH]
                    for ic in range(IC):
                        P.do("tensor", mk("matmul", dn_bank,
                                          wd_sb[wpar][:, ic * 128:(ic + 1) * 128],
                                          h_sb[:, ic * TCH:(ic + 1) * TCH],
                                          start=(ic == 0), stop=(ic == IC - 1)),
                             wait=(("h_v", h_here), (f"wdl{wpar}", wd_here),
                                   ("ye_v", max(0, V("ye_v") - 1)),),
                             inc="dn_mm" if ic == IC - 1 else None)
                    dn_hist[ch].append(V("dn_mm"))
                    P.do("vector", mk("tensor_copy", yt_sb[:, dt * TCH:(dt + 1) * TCH], dn_bank),
                         wait=(("dn_mm", V("dn_mm")), ("yt_mm", yt_prev_chunk)),
                         inc="ye_v", ss=True)
                ye_here = V("ye_v")
                sc_prev_chunk = V("sc_dma")
                for tt in range(TPC):
                    for dt in range(DC):
                        P.do("tensor", mk("transpose", p45[:, dt * 128:(dt + 1) * 128],
                                          yt_sb[:, dt * TCH + tt * 128: dt * TCH + tt * 128 + 128],
                                          ident[:, :]),
                             wait=(("ye_v", ye_here), ("ysc_v", V("ysc_v")),),
                             inc="yt_mm" if dt == DC - 1 else None)
                    P.do("vector", mk("tensor_scalar", y_sb[:, tt * D:(tt + 1) * D],
                                      p45[:, 0:D], ga_sb[:, tt:tt + 1], None, Alu.mult),
                         wait=(("yt_mm", V("yt_mm")), ("ga_v", V("ga_v")),
                               ("sc_dma", sc_prev_chunk)),
                         inc="ysc_v", ss=True)
                ysc_here = V("ysc_v")
                P.do("gpsimd", mk_scatter(icol0),
                     wait=(("ysc_v", ysc_here), ("out_st", out_rows_done),),
                     inc="sc_dma", incn=16)

    if pend >= 4:
        P.wait("gpsimd", "sc_dma", V("sc_dma"))
        P.wait("scalar", "out_st", V("out_st"))
    if pend == 2:
        P.wait("gpsimd", "rep_st", V("rep_st"))
    if pend == 3:
        P.wait("scalar", "out_st", V("out_st"))
    if debug:
        P.wait("gpsimd", "dbg_g", V("dbg_g"))

    sem_names = []
    for e2 in ENGINES:
        for w, _, post in P.items[e2]:
            for s, _v in w:
                if s not in sem_names:
                    sem_names.append(s)
            for ps, _pn in post:
                if ps not in sem_names:
                    sem_names.append(ps)
    import contextlib
    stack = contextlib.ExitStack()
    sems = {s: stack.enter_context(nc.semaphore(s)) for s in sem_names}

    def body(eng_name):
        def run(eng):
            for w, fn, post in P.items[eng_name]:
                for s, v in w:
                    eng.wait_ge(sems[s], v)
                if fn is None:
                    continue
                inst = fn(eng)
                for ps, pn in post:
                    inst.then_inc(sems[ps], pn)
        return run

    with nc.Block() as blk:
        blk.sync(body("sync"))
        blk.vector(body("vector"))
        blk.scalar(body("scalar"))
        blk.tensor(body("tensor"))
        blk.gpsimd(body("gpsimd"))
    stack.close()
    return nc


def _stage_inputs(inputs):
    x = np.ascontiguousarray(np.asarray(inputs["L0_attn_x"], dtype=np.float32))
    wqkv = np.ascontiguousarray(np.asarray(inputs["L0_attn_W_qkv"], dtype=np.float32))
    bqkv = np.ascontiguousarray(np.asarray(inputs["L0_attn_b_qkv"], dtype=np.float32))
    wout = np.ascontiguousarray(np.asarray(inputs["L0_attn_W_out"], dtype=np.float32))
    bout = np.ascontiguousarray(np.asarray(inputs["L0_attn_b_out"], dtype=np.float32))
    gw = np.asarray(inputs["L0_ffn_gate_w"], dtype=np.float32)
    wg = np.asarray(inputs["L0_ffn_w_gate"], dtype=np.float32)
    wu = np.asarray(inputs["L0_ffn_w_up"], dtype=np.float32)
    wd = np.asarray(inputs["L0_ffn_w_down"], dtype=np.float32)
    E, I, D = wg.shape
    gwT = np.ascontiguousarray(gw.T)
    IC, DC = I // 128, D // 128
    wgT = np.ascontiguousarray(wg.reshape(E, IC, 128, DC, 128).transpose(0, 1, 4, 3, 2))
    wuT = np.ascontiguousarray(wu.reshape(E, IC, 128, DC, 128).transpose(0, 1, 4, 3, 2))
    wdT = np.ascontiguousarray(wd.reshape(E, DC, 128, IC, 128).transpose(0, 1, 4, 3, 2))
    B, S, _ = x.shape
    per_core = []
    for c in range(8):
        b, h = c // 2, c % 2
        own = x[b, (S // 2) * h:(S // 2) * (h + 1)]
        oth = x[b, (S // 2) * (1 - h):(S // 2) * (2 - h)]
        xfc = np.ascontiguousarray(np.concatenate([own, oth], axis=0))
        per_core.append({
            "xf": xfc, "wqkv": wqkv, "bqkv": bqkv, "wout": wout, "bout": bout,
            "gwT": gwT, "wgT": wgT, "wuT": wuT, "wdT": wdT,
        })
    return per_core, (B, S, D)


def kernel(**inputs):
    cfg = Cfg()
    in_maps, (B, S, D) = _stage_inputs(inputs)
    nc = build_program(cfg)
    mybir.codegen_inst_isa_subclasses(nc)
    res = run_bass_kernel_spmd(nc, in_maps, core_ids=list(range(8)))
    outf = np.empty((B, S, D), dtype=np.float32)
    for c in range(8):
        b, h = c // 2, c % 2
        outf[b, (S // 2) * h:(S // 2) * (h + 1)] = res.results[c]["out"][:S // 2]
    return outf


# version 06
# speedup vs baseline: 1.5020x; 1.2372x over previous
import math
import numpy as np

import concourse.bass as bass
import concourse.mybir as mybir
from concourse.bass_utils import run_bass_kernel_spmd
from concourse.bass_isa import ReduceOp
from concourse import library_config as libcfg

F32 = mybir.dt.float32
I16 = mybir.dt.int16
U32 = mybir.dt.uint32
Alu = mybir.AluOpType
AX = mybir.AxisListType
Act = mybir.ActivationFunctionType

ENGINES = ("sync", "vector", "scalar", "tensor", "gpsimd")


class Cfg:
    def __init__(self, D=1024, S=8192, E=8, I=4096, CAP=1152, TCH=384,
                 KB=512, NITER=56, SHIFT=12.0, EPS=1e-5):
        self.D, self.S, self.E, self.I = D, S, E, I
        self.SO = S // 2
        self.CAP, self.TCH = CAP, TCH
        self.KB = KB
        self.NITER = NITER
        self.SHIFT = SHIFT
        self.EPS = EPS
        self.DC = D // 128
        self.IC = I // 128
        self.NT = S // 128
        self.NCH = S // 512
        self.NKB = S // KB
        self.NTCH = CAP // TCH
        self.KSEL = 2 * S // E
        nqt = self.SO // 128
        self.QCS = []
        while nqt > 0:
            t = min(3, nqt)
            self.QCS.append(t)
            nqt -= t


class Prog:

    def __init__(self):
        self.items = {e: [] for e in ENGINES}
        self.counts = {}
        self.last_wait = {e: {} for e in ENGINES}

    def val(self, sem):
        return self.counts.get(sem, 0)

    def do(self, eng, fn, wait=(), inc=None, incn=1, ss=False):
        w = []
        for s, v in wait:
            if v <= 0:
                continue
            if self.last_wait[eng].get(s, -1) >= v:
                continue
            self.last_wait[eng][s] = v
            w.append((s, v))
        post = []
        if inc is not None:
            self.counts[inc] = self.counts.get(inc, 0) + incn
            post.append((inc, incn))
        if ss and inc is None:
            sn = "ss_" + eng
            self.counts[sn] = self.counts.get(sn, 0) + 1
            post.append((sn, 1))
        self.items[eng].append((tuple(w), fn, tuple(post)))
        if ss:
            sw, svv = post[-1][0], self.counts[post[-1][0]]
            self.items[eng].append((((sw, svv),), None, ()))
            self.last_wait[eng][sw] = svv
        return self.counts.get(inc, 0) if inc else None

    def wait(self, eng, sem, v):
        self.do(eng, None, wait=((sem, v),))


def build_program(cfg, debug=False, pend=4):
    nc = bass.Bass("TRN2", target_bir_lowering=False, detect_race_conditions=False)
    D, S, E, I = cfg.D, cfg.S, cfg.E, cfg.I
    SO, CAP, TCH, KB = cfg.SO, cfg.CAP, cfg.TCH, cfg.KB
    DC, IC, NT, NCH, NKB, NTCH = cfg.DC, cfg.IC, cfg.NT, cfg.NCH, cfg.NKB, cfg.NTCH
    F = 3 * D
    JH = SO // 128
    JF = NT
    ICOLS = CAP // 16
    ICC = TCH // 16
    TPC = TCH // 128
    NKC = KB // 128
    NHV = max(1, D // 512)
    NW = min(512, D)

    xf = nc.dram_tensor("xf", [S, D], F32, kind="ExternalInput").ap()
    wqkv = nc.dram_tensor("wqkv", [D, F], F32, kind="ExternalInput").ap()
    bqkv = nc.dram_tensor("bqkv", [F], F32, kind="ExternalInput").ap()
    wout = nc.dram_tensor("wout", [D, D], F32, kind="ExternalInput").ap()
    bout = nc.dram_tensor("bout", [D], F32, kind="ExternalInput").ap()
    gwT = nc.dram_tensor("gwT", [D, E], F32, kind="ExternalInput").ap()
    wgT = nc.dram_tensor("wgT", [E, IC, 128, DC, 128], F32, kind="ExternalInput").ap()
    wuT = nc.dram_tensor("wuT", [E, IC, 128, DC, 128], F32, kind="ExternalInput").ap()
    wdT = nc.dram_tensor("wdT", [E, DC, 128, IC, 128], F32, kind="ExternalInput").ap()
    out = nc.dram_tensor("out", [SO + 128, D], F32, kind="ExternalOutput").ap()

    hbm_kind = "ExternalOutput" if debug else "Internal"
    xn_h = nc.dram_tensor("xn_h", [S, D], F32, kind=hbm_kind).ap()
    af_h = nc.dram_tensor("af_h", [S, 64], F32, kind=hbm_kind).ap()
    qT_h = nc.dram_tensor("qT_h", [DC, 128, SO], F32, kind=hbm_kind).ap()
    kT_h = nc.dram_tensor("kT_h", [DC, 128, S], F32, kind=hbm_kind).ap()
    v_h = nc.dram_tensor("v_h", [S, D], F32, kind=hbm_kind).ap()
    if debug:
        dbg_aff = nc.dram_tensor("dbg_aff", [128, E * JF], F32, kind="ExternalOutput").ap()
        dbg_lo = nc.dram_tensor("dbg_lo", [128, E], F32, kind="ExternalOutput").ap()
        dbg_nf = nc.dram_tensor("dbg_nf", [1, E], U32, kind="ExternalOutput").ap()
        dbg_idx = nc.dram_tensor("dbg_idx", [128, E * ICOLS], I16, kind="ExternalOutput").ap()
        dbg_st1 = nc.dram_tensor("dbg_st1", [128, 5 * NT], F32, kind="ExternalOutput").ap()
        dbg_bi = nc.dram_tensor("dbg_bi", [128, 3 * E], F32, kind="ExternalOutput").ap()
        dbg_cx = nc.dram_tensor("dbg_cx", [2, 128, 3 * D], F32, kind="ExternalOutput").ap()
        dbg_SE = nc.dram_tensor("dbg_SE", [2, 128, KB], F32, kind="ExternalOutput").ap()
        dbg_el = nc.dram_tensor("dbg_el", [2, 128, 3 * 17], F32, kind="ExternalOutput").ap()

    A = nc.alloc_sbuf_tensor
    ident = A("ident", [128, 128], F32).ap()
    onesc = A("onesc", [1, 128], F32).ap()
    scols = A("scols", [128, NT], F32).ap()
    qcols = A("qcols", [128, NT], F32).ap()
    mcols = A("mcols", [128, NT], F32).ap()
    icols = A("icols", [128, NT], F32).ap()
    tmp1 = A("tmp1", [128, max(NT, 8)], F32).ap()
    tmp2 = A("tmp2", [128, max(NT, 8)], F32).ap()
    aff_tm = A("aff_tm", [128, E * JF], F32).ap()
    bqk_sb = A("bqk_sb", [128, 2 * DC], F32).ap()
    bv_sb = A("bv_sb", [1, D], F32).ap()
    bo_sb = A("bo_sb", [1, D], F32).ap()
    gw_sb = A("gw_sb", [128, DC * E], F32).ap()
    lo = A("lo", [128, E], F32).ap()
    hi = A("hi", [128, E], F32).ap()
    mid = A("mid", [128, E], F32).ap()
    cnt = A("cnt", [128, E], F32).ap()
    pred = A("pred", [128, E], F32).ap()
    dd1 = A("dd1", [128, E], F32).ap()
    msk = A("msk", [128, E * JF], F32).ap()
    iota1 = A("iota1", [128, E * JH], F32).ap()
    mif = A("mif", [128, E * JH], F32).ap()
    mi16 = A("mi16", [128, E * JH], I16).ap()
    wrap = A("wrap", [16, E * 8 * JH], I16).ap()
    idxs = A("idxs", [128, E * ICOLS], I16).ap()
    gidx = A("gidx", [128, E * ICOLS], I16).ap()
    sidx = A("sidx", [128, E * ICOLS], I16).ap()
    smsk = A("smsk", [128, E * ICOLS], I16).ap()
    nfound = A("nfound", [16, E], U32).ap()
    sio = A("sio", [16, ICOLS], F32).ap()
    Lcol = A("Lcol", [128, 3 * 16], F32).ap()
    ell = A("ell", [128, 3], F32).ap()
    sq_jnk = A("sq_jnk", [128, 1024], F32).ap()
    shf_sb = A("shf_sb", [128, 1], F32).ap()

    AR = 176 * 1024
    arena = A("arena", [128, AR], mybir.dt.uint8)
    abase = nc.lookup_mloc(arena).addr

    def at(name, shape, dtype, off):
        assert off + int(np.prod(shape[1:])) * mybir.dt.size(dtype) <= AR, name
        return nc.alloc_sbuf_tensor_at(name, shape, dtype, offset=abase + off).ap()

    fb = 4
    o = 0
    wq_sb = at("wq_sb", [128, DC * F], F32, o); o += DC * F * fb
    xc_sb = []
    for b in range(2):
        xc_sb.append(at(f"xc{b}", [128, 4 * D], F32, o)); o += 4 * D * fb
    xn_sb = at("xn_sb", [128, 4 * D], F32, o); o += 4 * D * fb
    xnT_sb = at("xnT_sb", [128, DC * 512], F32, o); o += DC * 512 * fb
    qke_sb = []
    for b in range(2):
        qke_sb.append(at(f"qke{b}", [128, 512], F32, o)); o += 512 * fb
    ve_sb = []
    for b in range(2):
        ve_sb.append(at(f"ve{b}", [128, D], F32, o)); o += D * fb
    afp_sb = at("afp_sb", [128, 4 * 64], F32, o); o += 4 * 64 * fb
    afc_sb = at("afc_sb", [128, 4 * E], F32, o); o += 4 * E * fb

    o = 0
    wo_sb = at("wo_sb", [128, DC * D], F32, o); o += DC * D * fb
    qt_sb = at("qt_sb", [128, DC * 384], F32, o); o += DC * 384 * fb
    kt_sb = []
    for b in range(2):
        kt_sb.append(at(f"kt{b}", [128, DC * KB], F32, o)); o += DC * KB * fb
    vb_sb = []
    for b in range(2):
        vb_sb.append(at(f"vb{b}", [128, NKC * D], F32, o)); o += NKC * D * fb
    E_sb = at("E_sb", [128, 3 * KB], F32, o); o += 3 * KB * fb
    et_sb = at("et_sb", [128, NKC * 384], F32, o); o += NKC * 384 * fb
    cx_sb = at("cx_sb", [128, 3 * D], F32, o); o += 3 * D * fb
    cxT_sb = at("cxT_sb", [128, DC * 384], F32, o); o += DC * 384 * fb
    xr_sb = at("xr_sb", [128, 3 * D], F32, o); o += 3 * D * fb
    ou_sb = at("ou_sb", [128, 3 * D], F32, o); o += 3 * D * fb

    o = 0
    gb_sb = []
    for b in range(2):
        gb_sb.append(at(f"gb{b}", [128, TPC * D], F32, o)); o += TPC * D * fb
    ab_sb = []
    for b in range(2):
        ab_sb.append(at(f"ab{b}", [128, TPC * 64], F32, o)); o += TPC * 64 * fb
    xg_sb = at("xg_sb", [128, DC * TCH], F32, o); o += DC * TCH * fb
    wg_sb = []
    for b in range(2):
        wg_sb.append(at(f"wg{b}", [128, DC * 128], F32, o)); o += DC * 128 * fb
    wu_sb = []
    for b in range(2):
        wu_sb.append(at(f"wu{b}", [128, DC * 128], F32, o)); o += DC * 128 * fb
    h_sb = at("h_sb", [128, IC * TCH], F32, o); o += IC * TCH * fb
    wd_sb = []
    for b in range(2):
        wd_sb.append(at(f"wd{b}", [128, IC * 128], F32, o)); o += IC * 128 * fb
    yt_sb = at("yt_sb", [128, DC * TCH], F32, o); o += DC * TCH * fb
    y_sb = at("y_sb", [128, TPC * D], F32, o); o += TPC * D * fb
    ga_sb = at("ga_sb", [128, TPC], F32, o); o += TPC * fb

    p01 = nc.alloc_psum_tensor("p01", [128, 1024], F32).ap()
    p23 = nc.alloc_psum_tensor("p23", [128, 1024], F32).ap()
    p45 = nc.alloc_psum_tensor("p45", [128, 1024], F32).ap()
    p67 = nc.alloc_psum_tensor("p67", [128, 1024], F32).ap()

    P = Prog()
    V = P.val

    def mk(meth, *args, **kw):
        def fn(e):
            try:
                return getattr(e, meth)(*args, **kw)
            except Exception as ex:
                raise RuntimeError(f"emit failed: {meth} args={[getattr(a,'shape',a) for a in args]}") from ex
        return fn

    P.do("gpsimd", mk("load_library", libcfg.standard))
    P.do("gpsimd", mk("memset", ident[:, :], 1.0))
    P.do("gpsimd", mk("affine_select", ident[:, :], ident[:, :], [[-1, 128]],
                      Alu.is_equal, 0.0, base=0, channel_multiplier=1), inc="id_g")
    P.do("gpsimd", mk("iota", iota1[:, :], [[0, E], [128, JH]], base=1,
                      channel_multiplier=1, allow_small_or_imprecise_dtypes=True),
         inc="id_g")
    P.do("gpsimd", mk("iota", sio[:, :], [[16, ICOLS]], base=0,
                      channel_multiplier=1, allow_small_or_imprecise_dtypes=True),
         inc="id_g")
    P.do("vector", mk("memset", onesc[:, :], 1.0), inc="c_v")
    P.do("vector", mk("memset", gidx[:, :], 0), inc="c_v")
    P.do("vector", mk("memset", sidx[:, :], 0), inc="c_v")
    P.do("vector", mk("memset", Lcol[:, :], 0.0), inc="c_v")
    P.do("vector", mk("memset", shf_sb[:, :], -cfg.SHIFT), inc="c_v")
    P.do("vector", mk("memset", afp_sb[:, :], 0.0), inc="c_v")
    P.do("vector", mk("memset", idxs[:, :], 0), inc="c_v")
    P.do("vector", mk("memset", smsk[:, :], 0), inc="c_v")
    P.do("sync", mk("dma_start", out=bqk_sb[:, :],
                    in_=bqkv[0:2 * D].rearrange("(ft p) -> p ft", p=128),
                    allow_slow_non_contiguous=True), inc="c_ld", incn=16)
    P.do("sync", mk("dma_start", out=bv_sb[:, :], in_=bqkv[2 * D:3 * D][None, :]), inc="c_ld", incn=16)
    P.do("sync", mk("dma_start", out=bo_sb[:, :], in_=bout[:][None, :]), inc="c_ld", incn=16)
    P.do("sync", mk("dma_start", out=gw_sb[:, :].rearrange("p (dc e) -> p dc e", e=E),
                    in_=gwT[:, :].rearrange("(dc p) e -> p dc e", p=128)), inc="c_ld", incn=16)
    P.do("sync", mk("dma_start", out=wq_sb[:, :].rearrange("p (dc f) -> p dc f", dc=DC),
                    in_=wqkv[:, :].rearrange("(dc p) f -> p dc f", p=128)), inc="w_ld", incn=16)

    for j in range(NT):
        b = j % 2
        xt = xc_sb[b][:, 0:D]
        xls = f"x0l{j % 2}"
        P.do("sync", mk("dma_start", out=xt, in_=xf[128 * j:128 * j + 128, :]),
             wait=(("st_v", max(0, j - 1)), ("st_s", max(0, j - 1))), inc=xls, incn=16)
        P.do("vector", mk("reduce_sum", scols[:, j:j + 1], xt, axis=AX.X),
             wait=((xls, V(xls)),), inc="st_v")
        P.do("scalar", mk("activation", sq_jnk[:, 0:D], xt, Act.Square,
                          bias=0.0, scale=1.0, accum_out=qcols[:, j:j + 1]),
             wait=((xls, V(xls)),), inc="st_s")
    st_all = V("st_v")
    sts_all = V("st_s")
    P.do("vector", mk("tensor_scalar", mcols[:, :], scols[:, :], 1.0 / D, None, Alu.mult),
         wait=(("st_v", st_all), ("st_s", sts_all)))
    P.do("vector", mk("tensor_scalar", tmp1[:, 0:NT], qcols[:, :], 1.0 / D, None, Alu.mult))
    P.do("vector", mk("tensor_tensor", tmp2[:, 0:NT], mcols[:, :], mcols[:, :], Alu.mult))
    P.do("vector", mk("tensor_tensor", tmp1[:, 0:NT], tmp1[:, 0:NT], tmp2[:, 0:NT], Alu.subtract))
    P.do("vector", mk("tensor_scalar", tmp1[:, 0:NT], tmp1[:, 0:NT], cfg.EPS, None, Alu.add),
         inc="var_v")
    P.do("scalar", mk("activation", tmp2[:, 0:NT], tmp1[:, 0:NT], Act.Sqrt, bias=0.0, scale=1.0),
         wait=(("var_v", 1),), inc="sqr_sc")
    P.do("vector", mk("reciprocal", icols[:, :], tmp2[:, 0:NT]), wait=(("sqr_sc", 1),),
         inc="vself")
    P.wait("vector", "vself", V("vself"))
    for _ in range(2):
        P.do("vector", mk("tensor_tensor", tmp2[:, 0:NT], icols[:, :], icols[:, :], Alu.mult))
        P.do("vector", mk("tensor_tensor", tmp2[:, 0:NT], tmp2[:, 0:NT], tmp1[:, 0:NT], Alu.mult))
        P.do("vector", mk("tensor_scalar", tmp2[:, 0:NT], tmp2[:, 0:NT], -0.5, 1.5, Alu.mult, Alu.add))
        P.do("vector", mk("tensor_tensor", icols[:, :], icols[:, :], tmp2[:, 0:NT], Alu.mult))
    P.do("vector", mk("engine_nop"), inc="is_v")
    if debug:
        for di, src_t in enumerate((scols, qcols, mcols, icols, tmp1)):
            P.do("scalar", mk("dma_start", out=dbg_st1[:, di * NT:(di + 1) * NT],
                              in_=src_t[:, 0:NT]),
                 wait=(("is_v", 1),), inc="dbg_st", incn=16)
        P.wait("scalar", "dbg_st", V("dbg_st"))

    if pend >= 1:
        xnst_hist = [0] * (NCH + 1)
        qst_hist = []
        vst_hist = []
        for ci in range(NCH):
            cb = ci % 2
            own = (512 * ci) < SO
            for jj in range(4):
                j = 4 * ci + jj
                P.do("sync", mk("dma_start", out=xc_sb[cb][:, jj * D:(jj + 1) * D],
                                in_=xf[128 * j:128 * j + 128, :]),
                     wait=(("st_v", st_all if ci <= 1 else 0),
                           ("st_s", sts_all if ci <= 1 else 0),
                           ("xn_v", max(0, 4 * (ci - 1))),), inc=f"xl{ci % 2}", incn=16)
            tp_before = V("tp_mm")
            for jj in range(4):
                j = 4 * ci + jj
                P.do("vector", mk("tensor_scalar", xn_sb[:, jj * D:(jj + 1) * D],
                                  xc_sb[cb][:, jj * D:(jj + 1) * D],
                                  mcols[:, j:j + 1], icols[:, j:j + 1],
                                  Alu.subtract, Alu.mult),
                     wait=((f"xl{ci % 2}", V(f"xl{ci % 2}")), ("is_v", 1),
                           ("tp_mm", tp_before), ("xnst", xnst_hist[ci - 1] if ci > 0 else 0)),
                     inc="xn_v", ss=True)
                P.do("scalar", mk("dma_start", out=xn_h[128 * j:128 * j + 128, :],
                                  in_=xn_sb[:, jj * D:(jj + 1) * D]),
                     wait=(("xn_v", V("xn_v")),), inc="xnst", incn=16)
            xnst_hist[ci] = V("xnst")
            lg_ev_prev = V("lg_ev")
            lg_mm_prev = V("lg_mm")
            for dc in range(DC):
                hb = dc % 2
                bank = p01[:, hb * 512: hb * 512 + 512]
                for jj in range(4):
                    P.do("tensor", mk("transpose", bank[:, jj * 128:(jj + 1) * 128],
                                      xn_sb[:, jj * D + dc * 128: jj * D + dc * 128 + 128],
                                      ident[:, :]),
                         wait=(("xn_v", 4 * ci + 4), ("id_g", 1),
                               ("te_v", max(0, V("te_v") - 1)),
                               ("ex_sc", V("ex_sc") if dc <= 1 else 0),
                               ),
                         inc="tp_mm" if jj == 3 else None)
                P.do("vector", mk("tensor_copy", xnT_sb[:, dc * 512:(dc + 1) * 512], bank),
                     wait=(("tp_mm", V("tp_mm")), ("lg_mm", lg_mm_prev)),
                     inc="te_v", ss=True)
            te_done = V("te_v")
            fts = (list(range(2 * DC)) if own else list(range(DC, 2 * DC)))
            for gi, ft in enumerate(fts):
                hb = gi % 2
                bank = p23[:, hb * 512: hb * 512 + 512]
                for dc in range(DC):
                    P.do("tensor", mk("matmul", bank,
                                      wq_sb[:, dc * F + ft * 128: dc * F + ft * 128 + 128],
                                      xnT_sb[:, dc * 512:(dc + 1) * 512],
                                      start=(dc == 0), stop=(dc == DC - 1)),
                         wait=(("te_v", te_done), ("w_ld", 16), ("c_ld", 64),
                               ("qe_v", max(0, V("qe_v") - 1))),
                         inc="qk_mm" if dc == DC - 1 else None)
                g = len(qst_hist)
                eb = qke_sb[gi % 2]
                P.do("vector", mk("tensor_scalar", eb[:, :], bank, bqk_sb[:, ft:ft + 1],
                                  None, Alu.add),
                     wait=(("qk_mm", V("qk_mm")),
                           ((f"qs{g % 2}"), V(f"qs{g % 2}"))),
                     inc="qe_v", ss=True)
                if ft < DC:
                    dst = qT_h[ft, :, 512 * ci: 512 * ci + 512]
                else:
                    dst = kT_h[ft - DC, :, 512 * ci: 512 * ci + 512]
                P.do("scalar", mk("dma_start", out=dst, in_=eb[:, :]),
                     wait=(("qe_v", V("qe_v")),), inc=f"qs{g % 2}", incn=16)
                qst_hist.append(0)
            for jj in range(4):
                j = 4 * ci + jj
                vt = 4 * ci + jj
                vp = [p45, p67][vt % 2]
                for dc in range(DC):
                    for hv in range(NHV):
                        P.do("tensor", mk("matmul", vp[:, hv * 512:hv * 512 + NW],
                                          xnT_sb[:, dc * 512 + jj * 128: dc * 512 + jj * 128 + 128],
                                          wq_sb[:, dc * F + 2 * D + hv * 512: dc * F + 2 * D + hv * 512 + NW],
                                          start=(dc == 0), stop=False),
                             wait=(("ve_v", max(0, V("ve_v") - 1)),))
                for hv in range(NHV):
                    P.do("tensor", mk("matmul", vp[:, hv * 512:hv * 512 + NW],
                                      onesc[:, :], bv_sb[:, hv * 512:hv * 512 + NW],
                                      start=False, stop=True),
                         inc="v_mm" if hv == NHV - 1 else None)
                g = len(vst_hist)
                vbuf = ve_sb[jj % 2]
                P.do("vector", mk("tensor_copy", vbuf[:, 0:D], vp[:, 0:D]),
                     wait=(("v_mm", V("v_mm")),
                           ((f"vs{jj % 2}"), V(f"vs{jj % 2}"))),
                     inc="ve_v", ss=True)
                P.do("scalar", mk("dma_start", out=v_h[128 * j:128 * j + 128, :], in_=vbuf[:, 0:D]),
                     wait=(("ve_v", V("ve_v")),), inc=f"vs{jj % 2}", incn=16)
                vst_hist.append(0)
                lg = p01[:, jj * E: jj * E + E]
                for dc in range(DC):
                    P.do("tensor", mk("matmul", lg,
                                      xnT_sb[:, dc * 512 + jj * 128: dc * 512 + jj * 128 + 128],
                                      gw_sb[:, dc * E:(dc + 1) * E],
                                      start=(dc == 0), stop=(dc == DC - 1)),
                         wait=(("te_v", te_done), ("ex_sc", V("ex_sc")),),
                         inc="lg_mm" if dc == DC - 1 else None)
                P.do("vector", mk("reduce_max", tmp1[:, 0:1], lg, axis=AX.X),
                     wait=(("lg_mm", V("lg_mm")),), ss=True)
                P.do("vector", mk("tensor_scalar", tmp1[:, 0:1], tmp1[:, 0:1], -1.0, None, Alu.mult),
                     inc="mx_v", ss=True)
                P.do("scalar", mk("activation", afc_sb[:, jj * E:(jj + 1) * E], lg, Act.Exp,
                                  bias=tmp1[:, 0:1], scale=1.0, accum_out=tmp2[:, 0:1]),
                     wait=(("mx_v", V("mx_v")),), inc="ex_sc")
                P.do("vector", mk("reciprocal", tmp2[:, 1:2], tmp2[:, 0:1]),
                     wait=(("ex_sc", V("ex_sc")),), inc="vself")
                P.wait("vector", "vself", V("vself"))
                P.do("vector", mk("tensor_scalar", afc_sb[:, jj * E:(jj + 1) * E],
                                  afc_sb[:, jj * E:(jj + 1) * E], tmp2[:, 1:2], None, Alu.mult), ss=True)
                P.do("vector", mk("tensor_copy",
                                  aff_tm[:, j: j + (E - 1) * JF + 1: JF],
                                  afc_sb[:, jj * E:(jj + 1) * E]))
                P.do("vector", mk("tensor_copy", afp_sb[:, jj * 64: jj * 64 + E],
                                  afc_sb[:, jj * E:(jj + 1) * E]),
                     wait=(("afst", V("afst") if False else 0),), inc="lg_ev")
            P.do("scalar", mk("dma_start",
                              out=af_h[512 * ci:512 * ci + 512, :].rearrange("(a p) f -> p a f", p=128),
                              in_=afp_sb[:, :].rearrange("p (a f) -> p a f", f=64)),
                 wait=(("lg_ev", V("lg_ev")),), inc="afst", incn=16)

    p1_xnst = V("xnst")
    p1_qs0, p1_qs1 = V("qs0"), V("qs1")
    p1_vs0, p1_vs1 = V("vs0"), V("vs1")
    p1_afst = V("afst")
    p1_lgmm = V("lg_mm")
    p1_vmm = V("v_mm")
    p1_exsc = V("ex_sc")
    aff_done = V("lg_ev")
    if pend == 0:
        P.wait("vector", "st_v", V("st_v"))
        P.wait("scalar", "st_s", V("st_s"))
    if pend == 1:
        P.wait("scalar", "xnst", V("xnst"))
        P.wait("scalar", "qs0", V("qs0"))
        P.wait("scalar", "qs1", V("qs1"))
        P.wait("scalar", "vs0", V("vs0"))
        P.wait("scalar", "vs1", V("vs1"))
        P.wait("scalar", "afst", V("afst"))

    if pend >= 2:
        aff3 = aff_tm.rearrange("p (e j) -> p e j", e=E)
        msk3 = msk.rearrange("p (e j) -> p e j", e=E)
        P.do("gpsimd", mk("load_library", libcfg.attn))
        P.do("vector", mk("memset", lo[:, :], 0.0), wait=(("lg_ev", aff_done),))
        P.do("vector", mk("memset", hi[:, :], 1.0))
        for it in range(cfg.NITER):
            if it > 0:
                P.do("vector", mk("tensor_scalar", pred[:, :], cnt[:, :], float(cfg.KSEL), None, Alu.is_ge),
                     wait=(("rt_gv", it),), ss=True)
                P.do("vector", mk("tensor_tensor", dd1[:, :], mid[:, :], lo[:, :], Alu.subtract), ss=True)
                P.do("vector", mk("tensor_tensor", dd1[:, :], pred[:, :], dd1[:, :], Alu.mult), ss=True)
                P.do("vector", mk("tensor_tensor", lo[:, :], lo[:, :], dd1[:, :], Alu.add), ss=True)
                P.do("vector", mk("tensor_tensor", dd1[:, :], hi[:, :], mid[:, :], Alu.subtract), ss=True)
                P.do("vector", mk("tensor_tensor", dd1[:, :], pred[:, :], dd1[:, :], Alu.mult), ss=True)
                P.do("vector", mk("tensor_tensor", hi[:, :], mid[:, :], dd1[:, :], Alu.add), ss=True)
            P.do("vector", mk("tensor_tensor", mid[:, :], lo[:, :], hi[:, :], Alu.add), ss=True)
            P.do("vector", mk("tensor_scalar", mid[:, :], mid[:, :], 0.5, None, Alu.mult), ss=True)
            P.do("vector", mk("tensor_tensor", msk3[:, :, :], aff3[:, :, :],
                              mid.unsqueeze(2).broadcast_to([128, E, JF]), Alu.is_ge), ss=True)
            P.do("vector", mk("tensor_reduce", cnt[:, :], msk3[:, :, :], op=Alu.add, axis=AX.X),
                 inc="rt_vg")
            P.do("gpsimd", mk("partition_all_reduce", cnt[:, :], cnt[:, :], 128, ReduceOp.add),
                 wait=(("rt_vg", it + 1), ("id_g", 2)), inc="rt_gv")
        P.do("vector", mk("tensor_scalar", pred[:, :], cnt[:, :], float(cfg.KSEL), None, Alu.is_ge),
             wait=(("rt_gv", cfg.NITER),), ss=True)
        P.do("vector", mk("tensor_tensor", dd1[:, :], mid[:, :], lo[:, :], Alu.subtract), ss=True)
        P.do("vector", mk("tensor_tensor", dd1[:, :], pred[:, :], dd1[:, :], Alu.mult), ss=True)
        P.do("vector", mk("tensor_tensor", lo[:, :], lo[:, :], dd1[:, :], Alu.add), inc="lo_v", ss=True)
        affo3 = aff_tm.rearrange("p (e j) -> p e j", e=E)[:, :, 0:JH]
        mif3 = mif.rearrange("p (e j) -> p e j", e=E)
        P.do("vector", mk("tensor_tensor", mif3[:, :, :], affo3[:, :, :],
                          lo.unsqueeze(2).broadcast_to([128, E, JH]), Alu.is_ge), ss=True)
        P.do("vector", mk("tensor_tensor", mif[:, :], mif[:, :], iota1[:, :], Alu.mult),
             wait=(("id_g", 2),), ss=True)
        P.do("vector", mk("tensor_scalar", mif[:, :], mif[:, :], -1.0, None, Alu.add), ss=True)
        P.do("vector", mk("tensor_copy", mi16[:, :], mif[:, :]), inc="msk_v", ss=True)
        for g in range(8):
            src = mi16[16 * g:16 * g + 16, :].rearrange("p (e j) -> p e j", e=E)
            dst = wrap[0:16, :].rearrange("p (e g j) -> p e g j", e=E, g=8)[:, :, g, :]
            P.do("sync", mk("dma_start", out=dst, in_=src),
                 wait=(("msk_v", V("msk_v")),), inc="fold_st", incn=16)
        P.do("gpsimd", mk("load_library", libcfg.sparse_gather))
        for e in range(E):
            P.do("gpsimd", mk("sparse_gather", idxs[0:16, e * ICOLS:(e + 1) * ICOLS],
                              wrap[:, e * 8 * JH:(e + 1) * 8 * JH],
                              num_found=nfound[0:1, e:e + 1]),
                 wait=(("fold_st", V("fold_st")),), inc="sg_g")
        P.do("sync", mk("dma_start", out=nfound[1:2, :], in_=nfound[0:1, :]),
             wait=(("sg_g", V("sg_g")),), inc="nfb_st", incn=16)
        P.do("sync", mk("dma_start", out=nfound[2:4, :], in_=nfound[0:2, :]),
             wait=(("nfb_st", 16),), inc="nfb_st", incn=16)
        P.do("sync", mk("dma_start", out=nfound[4:8, :], in_=nfound[0:4, :]),
             wait=(("nfb_st", 32),), inc="nfb_st", incn=16)
        P.do("sync", mk("dma_start", out=nfound[8:16, :], in_=nfound[0:8, :]),
             wait=(("nfb_st", 48),), inc="nfb_st", incn=16)
        nf_f = cnt
        P.do("vector", mk("tensor_copy", nf_f[0:16, 0:E], nfound[:, :]),
             wait=(("nfb_st", 64), ("sg_g", V("sg_g")),), ss=True)
        vmq = mif[0:16, 0:ICOLS]
        ivf = mif[0:16, ICOLS:2 * ICOLS]
        for e in range(E):
            P.do("vector", mk("tensor_scalar", vmq, sio[:, :], nf_f[0:16, e:e + 1],
                              None, Alu.is_lt),
                 wait=(("id_g", 3),), ss=True)
            P.do("vector", mk("tensor_copy", ivf, idxs[0:16, e * ICOLS:(e + 1) * ICOLS]),
                 ss=True)
            P.do("vector", mk("tensor_scalar", ivf, ivf, 1.0, None, Alu.add), ss=True)
            P.do("vector", mk("tensor_tensor", ivf, ivf, vmq, Alu.mult), ss=True)
            P.do("vector", mk("tensor_scalar", ivf, ivf, -1.0, None, Alu.add), ss=True)
            P.do("vector", mk("tensor_copy", idxs[0:16, e * ICOLS:(e + 1) * ICOLS], ivf),
                 inc="cln_v", ss=True)
        P.do("vector", mk("tensor_scalar", gidx[0:16, :], idxs[0:16, :], 0, 0, Alu.max, Alu.add),
             wait=(("cln_v", V("cln_v")),), ss=True)
        P.do("vector", mk("tensor_scalar", smsk[0:16, :], idxs[0:16, :], 0, SO + 1, Alu.is_lt, Alu.mult), ss=True)
        P.do("vector", mk("tensor_tensor", sidx[0:16, :], idxs[0:16, :], smsk[0:16, :], Alu.add),
             inc="idx_v", ss=True)
        P.do("gpsimd", mk("dma_start", out=gidx[16:32, :], in_=gidx[0:16, :]),
             wait=(("idx_v", 1),), inc="rep_st", incn=16)
        P.do("gpsimd", mk("dma_start", out=sidx[16:32, :], in_=sidx[0:16, :]),
             inc="rep_st", incn=16)
        P.do("gpsimd", mk("load_library", libcfg.mlp))
        if debug:
            P.do("gpsimd", mk("dma_start", out=dbg_bi[:, 0:E], in_=cnt[:, :]), inc="dbg_g", incn=16)
            P.do("gpsimd", mk("dma_start", out=dbg_bi[:, E:2 * E], in_=hi[:, :]), inc="dbg_g", incn=16)
            P.do("gpsimd", mk("dma_start", out=dbg_bi[:, 2 * E:3 * E], in_=msk[:, 0:E]), inc="dbg_g", incn=16)
            P.do("gpsimd", mk("dma_start", out=dbg_aff[:, :], in_=aff_tm[:, :]), inc="dbg_g", incn=16)
            P.do("gpsimd", mk("dma_start", out=dbg_lo[:, :], in_=lo[:, :]),
                 wait=(("lo_v", 1),), inc="dbg_g", incn=16)
            P.do("gpsimd", mk("dma_start", out=dbg_nf[:, :], in_=nfound[0:1, :]), inc="dbg_g", incn=16)
            P.do("gpsimd", mk("dma_start", out=dbg_idx[:, :], in_=idxs[:, :]), inc="dbg_g", incn=16)


    if pend >= 3:
        P.do("sync", mk("dma_start", out=wo_sb[:, :].rearrange("p (dc f) -> p dc f", dc=DC),
                        in_=wout[:, :].rearrange("(dc p) f -> p dc f", p=128)),
             wait=(("lg_mm", p1_lgmm), ("v_mm", p1_vmm), ("ex_sc", p1_exsc),
                   ("lo_v", 1)),
             inc="w3_ld", incn=16)
        ctx_banks = [p01, p23, p45]
        kt_hist = []
        vbl_hist = []
        ou_slot_hist = {}
        q0 = 0
        for qi, nqt in enumerate(cfg.QCS):
            WQ = 128 * nqt
            P.do("sync", mk("dma_start", out=qt_sb[:, 0:DC * WQ].rearrange("p (dc c) -> p dc c", dc=DC),
                            in_=qT_h[:, :, q0:q0 + WQ].rearrange("dc p c -> p dc c")),
                 wait=(("qs0", p1_qs0), ("qs1", p1_qs1), ("xnst", p1_xnst),
                       ("vs0", p1_vs0), ("vs1", p1_vs1),
                       ("op_mm", V("op_mm"))),
                 inc="q_ld", incn=16)
            q_ld_here = V("q_ld")
            for kb in range(NKB):
                gkb = len(kt_hist)
                pb = gkb % 2
                P.do("sync", mk("dma_start",
                                out=kt_sb[pb][:, :].rearrange("p (dc c) -> p dc c", dc=DC),
                                in_=kT_h[:, :, KB * kb:KB * kb + KB].rearrange("dc p c -> p dc c")),
                     wait=(("ete_v", kt_hist[gkb - 2] if gkb >= 2 else 0),),
                     inc=f"ktl{pb}", incn=16)
                P.do("sync", mk("dma_start",
                                out=vb_sb[pb][:, :].rearrange("p (a f) -> p a f", a=NKC),
                                in_=v_h[KB * kb:KB * kb + KB, :].rearrange("(a p) f -> p a f", p=128)),
                     wait=(("cx_mm", vbl_hist[gkb - 2] if gkb >= 2 else 0),),
                     inc=f"vbl{pb}", incn=16)
                kt_here = V(f"ktl{pb}")
                vb_here = V(f"vbl{pb}")
                cte_prev = V("cte_v")
                for qt in range(nqt):
                    for dc in range(DC):
                        P.do("tensor", mk("matmul", p67[:, 0:KB],
                                          qt_sb[:, dc * WQ + qt * 128: dc * WQ + qt * 128 + 128],
                                          kt_sb[pb][:, dc * KB:(dc + 1) * KB],
                                          start=(dc == 0), stop=(dc == DC - 1)),
                             wait=(((f"ktl{pb}", kt_here), ("q_ld", q_ld_here),
                                    ("ex3_sc", V("ex3_sc")), ("cte_v", cte_prev))
                                   if dc == 0 else ()),
                             inc="s_mm" if dc == DC - 1 else None)
                    P.do("scalar", mk("activation", E_sb[:, qt * KB:(qt + 1) * KB],
                                      p67[:, 0:KB], Act.Exp, bias=shf_sb[:, 0:1],
                                      scale=1.0 / math.sqrt(D),
                                      accum_out=Lcol[:, qt * 16 + kb: qt * 16 + kb + 1]),
                         wait=(("s_mm", V("s_mm")), ("etm_mm", V("etm_mm")), ("c_v", 5)),
                         inc="ex3_sc", ss=True)
                    if debug and qi == 0 and kb == 0 and qt == 0:
                        P.do("scalar", mk("dma_start", out=dbg_SE[1, :, :],
                                          in_=E_sb[:, 0:KB]),
                             wait=(("ex3_sc", V("ex3_sc")),), inc="dbg_st", incn=16)
                ex_here = V("ex3_sc")
                for kc in range(NKC):
                    for qt in range(nqt):
                        P.do("tensor", mk("transpose",
                                          p67[:, 512 + qt * 128: 512 + qt * 128 + 128],
                                          E_sb[:, qt * KB + kc * 128: qt * KB + kc * 128 + 128],
                                          ident[:, :]),
                             wait=(("ex3_sc", ex_here), ("ete_v", V("ete_v")),),
                             inc="etm_mm" if qt == nqt - 1 else None)
                    P.do("vector", mk("tensor_copy", et_sb[:, kc * 384: kc * 384 + WQ],
                                      p67[:, 512:512 + WQ]),
                         wait=(("etm_mm", V("etm_mm")), ("cx_mm", V("cx_mm"))),
                         inc="ete_v", ss=True)
                    ete_here = V("ete_v")
                    for qt in range(nqt):
                        for hv in range(NHV):
                            P.do("tensor", mk("matmul",
                                              ctx_banks[qt][:, hv * 512: hv * 512 + NW],
                                              et_sb[:, kc * 384 + qt * 128: kc * 384 + qt * 128 + 128],
                                              vb_sb[pb][:, kc * D + hv * 512: kc * D + hv * 512 + NW],
                                              start=(kb == 0 and kc == 0),
                                              stop=(kb == NKB - 1 and kc == NKC - 1)),
                                 wait=((("ete_v", ete_here), (f"vbl{pb}", vb_here),
                                        ("oev_v", V("oev_v")))
                                       if (qt == 0 and hv == 0) else ()),
                                 inc="cx_mm" if (qt == nqt - 1 and hv == NHV - 1) else None)
                kt_hist.append(V("ete_v"))
                vbl_hist.append(V("cx_mm"))
            cx_done = V("cx_mm")
            for qt in range(nqt):
                P.do("vector", mk("reduce_sum", ell[:, qt:qt + 1],
                                  Lcol[:, qt * 16: qt * 16 + NKB], axis=AX.X),
                     wait=(("cx_mm", cx_done),), ss=True)
                P.do("vector", mk("reciprocal", ell[:, qt:qt + 1], ell[:, qt:qt + 1]),
                     inc="vself")
                P.wait("vector", "vself", V("vself"))
                P.do("vector", mk("tensor_scalar", cx_sb[:, qt * D:(qt + 1) * D],
                                  ctx_banks[qt][:, 0:D], ell[:, qt:qt + 1], None, Alu.mult),
                     inc="cb_v", ss=True)
            cb_here = V("cb_v")
            if debug and qi < 2:
                P.do("scalar", mk("dma_start", out=dbg_cx[qi, :, :], in_=cx_sb[:, 0:3 * D]),
                     wait=(("cb_v", V("cb_v")),), inc="dbg_st", incn=16)
                P.do("scalar", mk("dma_start", out=dbg_el[qi, :, 0:48],
                                  in_=Lcol[:, 0:48]), inc="dbg_st", incn=16)
                P.do("scalar", mk("dma_start", out=dbg_el[qi, :, 48:51],
                                  in_=ell[:, 0:3]), inc="dbg_st", incn=16)
            for dc in range(DC):
                hb = dc % 2
                bank = p67[:, hb * 512: hb * 512 + 512]
                for qt in range(nqt):
                    P.do("tensor", mk("transpose", bank[:, qt * 128:(qt + 1) * 128],
                                      cx_sb[:, qt * D + dc * 128: qt * D + dc * 128 + 128],
                                      ident[:, :]),
                         wait=(("cb_v", cb_here), ("cte_v", max(0, V("cte_v") - 1)),),
                         inc="ctm_mm" if qt == nqt - 1 else None)
                P.do("vector", mk("tensor_copy", cxT_sb[:, dc * 384: dc * 384 + WQ],
                                  bank[:, 0:WQ]),
                     wait=(("ctm_mm", V("ctm_mm")),), inc="cte_v", ss=True)
            cte_here = V("cte_v")
            P.do("sync", mk("dma_start",
                            out=xr_sb[:, 0:nqt * D].rearrange("p (a f) -> p a f", a=nqt),
                            in_=xf[q0:q0 + WQ, :].rearrange("(a p) f -> p a f", p=128)),
                 wait=(("oev_v", V("oev_v")),),
                 inc="res_ld", incn=16)
            res_here = V("res_ld")
            for qt in range(nqt):
                bank = ctx_banks[qt]
                for hv in range(NHV):
                    for dc in range(DC):
                        P.do("tensor", mk("matmul", bank[:, hv * 512: hv * 512 + NW],
                                          cxT_sb[:, dc * 384 + qt * 128: dc * 384 + qt * 128 + 128],
                                          wo_sb[:, dc * D + hv * 512: dc * D + hv * 512 + NW],
                                          start=(dc == 0), stop=False),
                             wait=((("cte_v", cte_here), ("w3_ld", 16),) if dc == 0 else ()))
                    P.do("tensor", mk("matmul", bank[:, hv * 512: hv * 512 + NW],
                                      onesc[:, :], bo_sb[:, hv * 512: hv * 512 + NW],
                                      start=False, stop=True),
                         inc="op_mm" if hv == NHV - 1 else None)
                P.do("vector", mk("tensor_tensor", ou_sb[:, qt * D:(qt + 1) * D],
                                  bank[:, 0:D], xr_sb[:, qt * D:(qt + 1) * D], Alu.add),
                     wait=(("op_mm", V("op_mm")), ("res_ld", res_here),
                           ("out_st", V("out_st"))),
                     inc="oev_v", ss=True)
                P.do("scalar", mk("dma_start", out=out[q0 + qt * 128: q0 + qt * 128 + 128, :],
                                  in_=ou_sb[:, qt * D:(qt + 1) * D]),
                     wait=(("oev_v", V("oev_v")),), inc="out_st", incn=16)
                ou_slot_hist[qt] = V("out_st")
            q0 += WQ
        out_rows_done = V("out_st")
        p3_opmm = V("op_mm")
        p3_oev = V("oev_v")
        p3_cte = V("cte_v")


    if pend >= 4:
        RREG = {}

        def _mkreg(e):
            RREG["t"] = e.to_reg(TCH)
            return None
        P.do("gpsimd", _mkreg)

        def mk_gather(dst, srch, icol0):
            def fn(e):
                return e.dma_gather(dst, srch, gidx[:, icol0:icol0 + ICC], TCH, RREG["t"],
                                    dst.shape[-1])
            return fn

        def mk_scatter(icol0):
            def fn(e):
                return e.dma_scatter_add(out[:, :], y_sb[:, :].rearrange("p (a f) -> p a f", f=D),
                                         sidx[:, icol0:icol0 + ICC], TCH, RREG["t"], D)
            return fn
        gt_hist = []
        ga_hist = []
        gu_hist = []
        dn_hist = []
        for e in range(E):
            for c in range(NTCH):
                ch = e * NTCH + c
                gpar = ch % 2
                icol0 = e * ICOLS + c * ICC
                gu_hist.append([])
                dn_hist.append([])
                P.do("gpsimd", mk_gather(gb_sb[gpar][:, :].rearrange("p (a f) -> p a f", f=D),
                                         xn_h[:, :], icol0),
                     wait=(("rep_st", 32), ("xnst", p1_xnst), ("afst", p1_afst),
                           ("op_mm", p3_opmm), ("oev_v", p3_oev),
                           ("gt_mm", gt_hist[ch - 2] if ch >= 2 else 0),
                           ("ga_v", ga_hist[ch - 2] if ch >= 2 else 0)),
                     inc=f"gd{gpar}", incn=16)
                P.do("gpsimd", mk_gather(ab_sb[gpar][:, :].rearrange("p (a f) -> p a f", f=64),
                                         af_h[:, :], icol0),
                     inc=f"gd{gpar}", incn=16)
                gds = f"gd{gpar}"
                g_here = V(gds)
                P.do("vector", mk("tensor_copy", ga_sb[:, 0:TPC],
                                  ab_sb[gpar][:, e: e + (TPC - 1) * 64 + 1: 64] if TPC > 1
                                  else ab_sb[gpar][:, e:e + 1]),
                     wait=((gds, g_here), ("ysc_v", V("ysc_v")),), inc="ga_v")
                ga_hist.append(V("ga_v"))
                gu_prev_chunk = V("gu_mm")
                for dc in range(DC):
                    hb = dc % 2
                    bank = p45[:, hb * 512: hb * 512 + 512]
                    for tt in range(TPC):
                        P.do("tensor", mk("transpose", bank[:, tt * 128:(tt + 1) * 128],
                                          gb_sb[gpar][:, tt * D + dc * 128: tt * D + dc * 128 + 128],
                                          ident[:, :]),
                             wait=((gds, g_here),
                                   ("gte_v", max(0, V("gte_v") - 1)),
                                   ("ysc_v", V("ysc_v")),),
                             inc="gt_mm" if tt == TPC - 1 else None)
                    P.do("vector", mk("tensor_copy", xg_sb[:, dc * TCH:(dc + 1) * TCH],
                                      bank[:, 0:TCH]),
                         wait=(("gt_mm", V("gt_mm")), ("gu_mm", gu_prev_chunk)),
                         inc="gte_v", ss=True)
                gt_hist.append(V("gt_mm"))
                gte_here = V("gte_v")
                dn_prev_chunk = V("dn_mm")
                for it in range(IC):
                    wpar = it % 2
                    P.do("sync", mk("dma_start",
                                    out=wg_sb[wpar][:, :].rearrange("p (dc i) -> p dc i", dc=DC),
                                    in_=wgT[e, it, :, :, :]),
                         wait=(("gu_mm", gu_hist[ch][it - 2] if it >= 2 else
                                (gu_hist[ch - 1][IC - 2 + it] if ch >= 1 else 0)),
                               ("oev_v", p3_oev), ("cte_v", p3_cte)),
                         inc=f"wt{wpar}", incn=16)
                    P.do("sync", mk("dma_start",
                                    out=wu_sb[wpar][:, :].rearrange("p (dc i) -> p dc i", dc=DC),
                                    in_=wuT[e, it, :, :, :]),
                         inc=f"wt{wpar}", incn=16)
                    wt_here = V(f"wt{wpar}")
                    gu_bank = [p01, p23][it % 2]
                    for dc in range(DC):
                        P.do("tensor", mk("matmul", gu_bank[:, 0:TCH],
                                          wg_sb[wpar][:, dc * 128:(dc + 1) * 128],
                                          xg_sb[:, dc * TCH:(dc + 1) * TCH],
                                          start=(dc == 0), stop=(dc == DC - 1)),
                             wait=((("gte_v", gte_here), (f"wt{wpar}", wt_here),
                                    ("h_v", max(0, V("h_v") - 1)),) if dc == 0 else ()),
                             inc="gu_mm" if dc == DC - 1 else None)
                    for dc in range(DC):
                        P.do("tensor", mk("matmul", gu_bank[:, 512:512 + TCH],
                                          wu_sb[wpar][:, dc * 128:(dc + 1) * 128],
                                          xg_sb[:, dc * TCH:(dc + 1) * TCH],
                                          start=(dc == 0), stop=(dc == DC - 1)),
                             inc="gu_mm" if dc == DC - 1 else None)
                    gu_hist[ch].append(V("gu_mm"))
                    P.do("scalar", mk("activation", h_sb[:, it * TCH:(it + 1) * TCH],
                                      gu_bank[:, 0:TCH], Act.Silu, bias=0.0, scale=1.0),
                         wait=(("gu_mm", V("gu_mm")), ("dn_mm", dn_prev_chunk)),
                         inc="h_sc", ss=True)
                    P.do("vector", mk("tensor_tensor", h_sb[:, it * TCH:(it + 1) * TCH],
                                      h_sb[:, it * TCH:(it + 1) * TCH],
                                      gu_bank[:, 512:512 + TCH], Alu.mult),
                         wait=(("h_sc", V("h_sc")),), inc="h_v", ss=True)
                h_here = V("h_v")
                yt_prev_chunk = V("yt_mm")
                for dt in range(DC):
                    wpar = dt % 2
                    P.do("sync", mk("dma_start",
                                    out=wd_sb[wpar][:, :].rearrange("p (ic d) -> p ic d", ic=IC),
                                    in_=wdT[e, dt, :, :, :]),
                         wait=(("dn_mm", dn_hist[ch][dt - 2] if dt >= 2 else
                                (dn_hist[ch - 1][DC - 2 + dt] if ch >= 1 else 0)),),
                         inc=f"wdl{wpar}", incn=16)
                    wd_here = V(f"wdl{wpar}")
                    dn_bank = p67[:, (dt % 2) * 512: (dt % 2) * 512 + TCH]
                    for ic in range(IC):
                        P.do("tensor", mk("matmul", dn_bank,
                                          wd_sb[wpar][:, ic * 128:(ic + 1) * 128],
                                          h_sb[:, ic * TCH:(ic + 1) * TCH],
                                          start=(ic == 0), stop=(ic == IC - 1)),
                             wait=((("h_v", h_here), (f"wdl{wpar}", wd_here),
                                    ("ye_v", max(0, V("ye_v") - 1)),) if ic == 0 else ()),
                             inc="dn_mm" if ic == IC - 1 else None)
                    dn_hist[ch].append(V("dn_mm"))
                    P.do("vector", mk("tensor_copy", yt_sb[:, dt * TCH:(dt + 1) * TCH], dn_bank),
                         wait=(("dn_mm", V("dn_mm")), ("yt_mm", yt_prev_chunk)),
                         inc="ye_v", ss=True)
                ye_here = V("ye_v")
                sc_prev_chunk = V("sc_dma")
                for tt in range(TPC):
                    for dt in range(DC):
                        P.do("tensor", mk("transpose", p45[:, dt * 128:(dt + 1) * 128],
                                          yt_sb[:, dt * TCH + tt * 128: dt * TCH + tt * 128 + 128],
                                          ident[:, :]),
                             wait=(("ye_v", ye_here), ("ysc_v", V("ysc_v")),),
                             inc="yt_mm" if dt == DC - 1 else None)
                    P.do("vector", mk("tensor_scalar", y_sb[:, tt * D:(tt + 1) * D],
                                      p45[:, 0:D], ga_sb[:, tt:tt + 1], None, Alu.mult),
                         wait=(("yt_mm", V("yt_mm")), ("ga_v", V("ga_v")),
                               ("sc_dma", sc_prev_chunk)),
                         inc="ysc_v", ss=True)
                ysc_here = V("ysc_v")
                P.do("gpsimd", mk_scatter(icol0),
                     wait=(("ysc_v", ysc_here), ("out_st", out_rows_done),),
                     inc="sc_dma", incn=16)

    if pend >= 4:
        P.wait("gpsimd", "sc_dma", V("sc_dma"))
        P.wait("scalar", "out_st", V("out_st"))
    if pend == 2:
        P.wait("gpsimd", "rep_st", V("rep_st"))
    if pend == 3:
        P.wait("scalar", "out_st", V("out_st"))
    if debug:
        P.wait("gpsimd", "dbg_g", V("dbg_g"))

    sem_names = []
    for e2 in ENGINES:
        for w, _, post in P.items[e2]:
            for s, _v in w:
                if s not in sem_names:
                    sem_names.append(s)
            for ps, _pn in post:
                if ps not in sem_names:
                    sem_names.append(ps)
    import contextlib
    stack = contextlib.ExitStack()
    sems = {s: stack.enter_context(nc.semaphore(s)) for s in sem_names}

    def body(eng_name):
        def run(eng):
            for w, fn, post in P.items[eng_name]:
                for s, v in w:
                    eng.wait_ge(sems[s], v)
                if fn is None:
                    continue
                inst = fn(eng)
                for ps, pn in post:
                    inst.then_inc(sems[ps], pn)
        return run

    with nc.Block() as blk:
        blk.sync(body("sync"))
        blk.vector(body("vector"))
        blk.scalar(body("scalar"))
        blk.tensor(body("tensor"))
        blk.gpsimd(body("gpsimd"))
    stack.close()
    return nc


def _stage_inputs(inputs):
    x = np.ascontiguousarray(np.asarray(inputs["L0_attn_x"], dtype=np.float32))
    wqkv = np.ascontiguousarray(np.asarray(inputs["L0_attn_W_qkv"], dtype=np.float32))
    bqkv = np.ascontiguousarray(np.asarray(inputs["L0_attn_b_qkv"], dtype=np.float32))
    wout = np.ascontiguousarray(np.asarray(inputs["L0_attn_W_out"], dtype=np.float32))
    bout = np.ascontiguousarray(np.asarray(inputs["L0_attn_b_out"], dtype=np.float32))
    gw = np.asarray(inputs["L0_ffn_gate_w"], dtype=np.float32)
    wg = np.asarray(inputs["L0_ffn_w_gate"], dtype=np.float32)
    wu = np.asarray(inputs["L0_ffn_w_up"], dtype=np.float32)
    wd = np.asarray(inputs["L0_ffn_w_down"], dtype=np.float32)
    E, I, D = wg.shape
    gwT = np.ascontiguousarray(gw.T)
    IC, DC = I // 128, D // 128
    wgT = np.ascontiguousarray(wg.reshape(E, IC, 128, DC, 128).transpose(0, 1, 4, 3, 2))
    wuT = np.ascontiguousarray(wu.reshape(E, IC, 128, DC, 128).transpose(0, 1, 4, 3, 2))
    wdT = np.ascontiguousarray(wd.reshape(E, DC, 128, IC, 128).transpose(0, 1, 4, 3, 2))
    B, S, _ = x.shape
    per_core = []
    for c in range(8):
        b, h = c // 2, c % 2
        own = x[b, (S // 2) * h:(S // 2) * (h + 1)]
        oth = x[b, (S // 2) * (1 - h):(S // 2) * (2 - h)]
        xfc = np.ascontiguousarray(np.concatenate([own, oth], axis=0))
        per_core.append({
            "xf": xfc, "wqkv": wqkv, "bqkv": bqkv, "wout": wout, "bout": bout,
            "gwT": gwT, "wgT": wgT, "wuT": wuT, "wdT": wdT,
        })
    return per_core, (B, S, D)


def kernel(**inputs):
    cfg = Cfg()
    in_maps, (B, S, D) = _stage_inputs(inputs)
    nc = build_program(cfg)
    mybir.codegen_inst_isa_subclasses(nc)
    res = run_bass_kernel_spmd(nc, in_maps, core_ids=list(range(8)))
    outf = np.empty((B, S, D), dtype=np.float32)
    for c in range(8):
        b, h = c // 2, c % 2
        outf[b, (S // 2) * h:(S // 2) * (h + 1)] = res.results[c]["out"][:S // 2]
    return outf
